# version 04
# baseline (speedup 1.0000x reference)
import contextlib
import numpy as np
import concourse.bass as bass
import concourse.mybir as mybir
from concourse.bass_utils import run_bass_kernel_spmd

F32 = mybir.dt.float32
BF16 = mybir.dt.bfloat16
I32 = mybir.dt.int32
AF = mybir.ActivationFunctionType
ALU = mybir.AluOpType
AX = mybir.AxisListType.X

P = 128
D = 2048
NDC = 16
T = 1024
NH = 16
EH = 128
IEX = 1408
NIC = 11
NPE = 12
ISH = 5632
SCALE_QK = float(1.0 / np.sqrt(128.0))
NEG_BIG = -100000.0
LN_EPS = 1e-5
HB = 512


class Sem:
    def __init__(self, st, nc, name):
        self.h = st.enter_context(nc.semaphore(name))
        self.n = 0

    def inc(self, instr, k=1):
        instr.then_inc(self.h, k)
        self.n += k
        return self.n


def build_program(n_sub=3, debug=False):
    nc = bass.Bass("TRN2", target_bir_lowering=False)
    uid_ctr = [0]
    def uid():
        uid_ctr[0] += 1
        return "_u%d" % uid_ctr[0]

    xT_d = nc.dram_tensor("xT", [D, T], F32, kind="ExternalInput").ap()
    qb_d = nc.dram_tensor("qbase", [1, 1], F32, kind="ExternalInput").ap()
    idx_d = nc.dram_tensor("kvidx", [P, 2 * NH], I32, kind="ExternalInput").ap()
    w1T_d = nc.dram_tensor("w1T", [8, D, 2 * IEX], F32, kind="ExternalInput").ap()
    w2T_d = nc.dram_tensor("w2T", [8, IEX, D], F32, kind="ExternalInput").ap()
    sguT_d = nc.dram_tensor("sguT", [D, 2 * ISH], F32, kind="ExternalInput").ap()
    sdT_d = nc.dram_tensor("sdT", [ISH, D], F32, kind="ExternalInput").ap()
    gw9T_d = nc.dram_tensor("gw9T", [D, 16], F32, kind="ExternalInput").ap()
    wq_d = nc.dram_tensor("wq", [NH, D, EH], F32, kind="ExternalInput").ap()
    wk_d = nc.dram_tensor("wk", [NH, D, EH], F32, kind="ExternalInput").ap()
    wv_d = nc.dram_tensor("wv", [NH, D, EH], F32, kind="ExternalInput").ap()
    wo_d = nc.dram_tensor("wo", [NH, EH, D], F32, kind="ExternalInput").ap()
    bqT_d = nc.dram_tensor("bqT", [P, NH], F32, kind="ExternalInput").ap()
    bkT_d = nc.dram_tensor("bkT", [P, NH], F32, kind="ExternalInput").ap()
    bvT_d = nc.dram_tensor("bvT", [P, NH], F32, kind="ExternalInput").ap()
    bo_d = nc.dram_tensor("bo", [P, NDC], F32, kind="ExternalInput").ap()
    y_d = nc.dram_tensor("y", [D, T], F32, kind="ExternalOutput").ap()
    dbg_d = (nc.dram_tensor("dbg", [P, 4096], F32, kind="ExternalOutput").ap()
             if debug else None)
    dbg2_d = (nc.dram_tensor("dbg2", [NH * P, T], F32, kind="ExternalOutput").ap()
              if debug else None)

    kvK_in = nc.dram_tensor("kvK_in", [D, T], BF16).ap()
    kvV_in = nc.dram_tensor("kvV_in", [D, T], BF16).ap()
    kvK_out = nc.dram_tensor("kvK_out", [8 * D, T], BF16).ap()
    kvV_out = nc.dram_tensor("kvV_out", [8 * D, T], BF16).ap()
    qstage_d = nc.dram_tensor("qstage", [D, T], BF16).ap()
    kstat_in = nc.dram_tensor("kstat_in", [1, NH], F32).ap()
    kstat_out = nc.dram_tensor("kstat_out", [1, NH], F32).ap()

    st = contextlib.ExitStack()
    sb = lambda name, shape, dt: st.enter_context(nc.sbuf_tensor(name, shape, dt))

    xr = sb("xr", [P, NDC * T], F32)
    xl = sb("xl", [P, NDC * T], BF16)
    ones_b = sb("ones_b", [P, P], BF16)
    ones_f = sb("ones_f", [1, P], F32)
    half_f = sb("half_f", [1, P], F32)
    half_b = sb("half_b", [1, P], BF16)
    ident_f = sb("ident_f", [P, P], F32)
    ident_b = sb("ident_b", [P, P], BF16)
    iota_p = sb("iota_p", [P, 1], F32)
    qbB = sb("qbB", [P, 1], F32)
    gw9 = sb("gw9", [P, NDC * 16], BF16)
    bq_s = sb("bq_s", [P, NH], F32)
    bk_s = sb("bk_s", [P, NH], F32)
    bv_s = sb("bv_s", [P, NH], F32)
    bo_s = sb("bo_s", [P, NDC], F32)
    qb_s = sb("qb_s", [1, 1], F32)
    idx_s = sb("idx_s", [P, 2 * NH], I32)
    iota_sml = sb("iota_sml", [P, P], F32)
    bigm = sb("bigm", [P, 3072], BF16)

    sems = {}
    for name in ["pe", "dve", "act", "pool", "hd", "cc", "out", "kv",
                 "w0", "w1", "w2", "w3", "w4", "w5"]:
        sems[name] = Sem(st, nc, "s_" + name)
    s_pe, s_dve, s_act = sems["pe"], sems["dve"], sems["act"]
    s_pool, s_hd, s_cc = sems["pool"], sems["hd"], sems["cc"]
    s_out, s_kv = sems["out"], sems["kv"]
    sw = [sems["w%d" % i] for i in range(6)]

    ENGS = [nc.tensor, nc.vector, nc.scalar, nc.gpsimd, nc.sync]

    def barrier():
        for eng in ENGS:
            for s in sems.values():
                if s.n:
                    eng.wait_ge(s.h, s.n)

    def chain(eng, sem, instr, k=1):
        sem.inc(instr, k)
        eng.wait_ge(sem.h, sem.n)

    i = nc.sync.dma_start(
        xr[:, :].rearrange("p (dc t) -> p dc t", t=T),
        xT_d.rearrange("(dc p) t -> p dc t", p=P))
    s_hd.inc(i, 16)
    for dst, src in [(qb_s[:, :], qb_d), (idx_s[:, :], idx_d),
                     (bq_s[:, :], bqT_d), (bk_s[:, :], bkT_d),
                     (bv_s[:, :], bvT_d)]:
        s_hd.inc(nc.sync.dma_start(dst, src), 16)
    s_pool.inc(nc.gpsimd.dma_start(bo_s[:, :], bo_d), 16)
    s_pool.inc(nc.gpsimd.dma_start(
        gw9[:, :].rearrange("p (dc w) -> p dc w", w=16),
        gw9T_d.rearrange("(dc p) w -> p dc w", p=P)), 16)
    chain(nc.gpsimd, s_pool,
          nc.gpsimd.iota(iota_p[:, :], pattern=[[0, 1]], base=0,
                         channel_multiplier=1,
                         allow_small_or_imprecise_dtypes=True))
    chain(nc.gpsimd, s_pool,
          nc.gpsimd.iota(iota_sml[:, :], pattern=[[1, P]], base=0,
                         channel_multiplier=0,
                         allow_small_or_imprecise_dtypes=True))
    nc.vector.memset(ones_b[:, :], 1.0)
    nc.vector.memset(ones_f[:, :], 1.0)
    nc.vector.memset(half_f[:, :], 0.5)
    nc.vector.memset(half_b[:, :], 0.5)
    nc.vector.wait_ge(s_pool.h, s_pool.n)
    chain(nc.vector, s_dve,
          nc.vector.tensor_scalar(ident_f[:, :], iota_sml[:, :], iota_p[:, 0:1],
                                  None, ALU.is_equal))
    chain(nc.vector, s_dve, nc.vector.tensor_copy(ident_b[:, :], ident_f[:, :]))
    with nc.psum_tensor("pqb2", [P, 512], F32) as pqb:
        nc.tensor.wait_ge(s_hd.h, s_hd.n)
        nc.tensor.wait_ge(s_dve.h, s_dve.n)
        mm = nc.tensor.matmul(pqb[:, 0:1], ones_f[0:1, :], qb_s[0:1, 0:1],
                              start=True, stop=True)
        s_pe.inc(mm)
        nc.vector.wait_ge(s_pe.h, s_pe.n)
        chain(nc.vector, s_dve, nc.vector.tensor_copy(qbB[:, :], pqb[:, 0:1]))
    with nc.sbuf_tensor("iota3k_tmp", [P, 3072], F32) as iota3k, \
         nc.sbuf_tensor("tcol_tmp", [P, 1], F32) as tcol:
        chain(nc.gpsimd, s_pool,
              nc.gpsimd.iota(iota3k[:, :], pattern=[[1, 3072]], base=-2048,
                             channel_multiplier=0,
                             allow_small_or_imprecise_dtypes=True))
        nc.vector.wait_ge(s_pool.h, s_pool.n)
        nc.vector.wait_ge(s_dve.h, s_dve.n)
        nc.vector.wait_ge(s_pe.h, s_pe.n)
        chain(nc.vector, s_dve,
              nc.vector.tensor_tensor(tcol[:, :], iota_p[:, :], qbB[:, :],
                                      ALU.subtract))
        chain(nc.vector, s_dve,
              nc.vector.tensor_scalar(bigm[:, :], iota3k[:, :], tcol[:, 0:1],
                                      NEG_BIG, ALU.is_lt, ALU.mult))
    barrier()

    def layer_norm():
        with contextlib.ExitStack() as ph:
            _u = uid()
            psb = lambda n_, s_, d_: ph.enter_context(nc.psum_tensor(n_ + _u, s_, d_))
            lsb = lambda n_, s_, d_: ph.enter_context(nc.sbuf_tensor(n_ + _u, s_, d_))
            p_s = [psb("pls%d" % i2, [P, HB], F32) for i2 in range(2)]
            p_q = [psb("plq%d" % i2, [P, HB], F32) for i2 in range(2)]
            xb_t = [lsb("lnxb%d" % i2, [P, T], BF16) for i2 in range(2)]
            sq_t = [lsb("lnsq%d" % i2, [P, T], BF16) for i2 in range(2)]
            mB = lsb("lnmB", [P, T], F32)
            v1 = lsb("lnv1", [P, T], F32)
            rstd = lsb("lnrstd", [P, T], F32)
            tt32 = [lsb("lnt%d" % i2, [P, T], F32) for i2 in range(4)]

            dve0, act0, pe0 = s_dve.n, s_act.n, s_pe.n
            for dc in range(NDC):
                b = dc % 2
                if dc >= 2:
                    nc.vector.wait_ge(s_pe.h, pe0 + 4 * (dc - 1))
                    nc.scalar.wait_ge(s_pe.h, pe0 + 4 * (dc - 1))
                s_dve.inc(nc.vector.tensor_copy(xb_t[b][:, :],
                                                xr[:, dc * T:(dc + 1) * T]))
                s_act.inc(nc.scalar.activation(sq_t[b][:, :],
                                               xr[:, dc * T:(dc + 1) * T],
                                               AF.Square))
                nc.tensor.wait_ge(s_dve.h, dve0 + dc + 1)
                nc.tensor.wait_ge(s_act.h, act0 + dc + 1)
                for half in range(2):
                    s_pe.inc(nc.tensor.matmul(
                        p_s[half][:, :], ones_b[:, :],
                        xb_t[b][:, half * HB:(half + 1) * HB],
                        start=(dc == 0), stop=(dc == NDC - 1)))
                    s_pe.inc(nc.tensor.matmul(
                        p_q[half][:, :], ones_b[:, :],
                        sq_t[b][:, half * HB:(half + 1) * HB],
                        start=(dc == 0), stop=(dc == NDC - 1)))
            nc.scalar.wait_ge(s_pe.h, s_pe.n)
            for half in range(2):
                sl = slice(half * HB, (half + 1) * HB)
                s_act.inc(nc.scalar.activation(mB[:, sl], p_s[half][:, :],
                                               AF.Copy, scale=1.0 / D))
                s_act.inc(nc.scalar.activation(v1[:, sl], p_q[half][:, :],
                                               AF.Copy, scale=1.0 / D))
            nc.vector.wait_ge(s_act.h, s_act.n)
            chain(nc.vector, s_dve,
                  nc.vector.tensor_tensor(tt32[0][:, :], mB[:, :], mB[:, :],
                                          ALU.mult))
            chain(nc.vector, s_dve,
                  nc.vector.tensor_tensor(v1[:, :], v1[:, :], tt32[0][:, :],
                                          ALU.subtract))
            chain(nc.vector, s_dve,
                  nc.vector.tensor_scalar_add(v1[:, :], v1[:, :], LN_EPS))
            nc.scalar.wait_ge(s_dve.h, s_dve.n)
            chain(nc.scalar, s_act,
                  nc.scalar.activation(tt32[1][:, :], v1[:, :], AF.Ln))
            chain(nc.scalar, s_act,
                  nc.scalar.activation(rstd[:, :], tt32[1][:, :], AF.Exp,
                                       scale=-0.5))
            nc.vector.wait_ge(s_act.h, s_act.n)
            nc.gpsimd.wait_ge(s_act.h, s_act.n)
            dcnt = {}
            pcnt = {}
            for dc in range(NDC):
                b = dc % 2
                if dc < 10:
                    if dc >= 2:
                        nc.vector.wait_ge(s_dve.h, dcnt[dc - 2])
                    s_dve.inc(nc.vector.tensor_tensor(
                        tt32[b][:, :], xr[:, dc * T:(dc + 1) * T],
                        mB[:, :], ALU.subtract))
                    nc.vector.wait_ge(s_dve.h, s_dve.n)
                    s_dve.inc(nc.vector.tensor_tensor(
                        xl[:, dc * T:(dc + 1) * T], tt32[b][:, :],
                        rstd[:, :], ALU.mult))
                    dcnt[dc] = s_dve.n
                else:
                    if dc >= 12:
                        nc.gpsimd.wait_ge(s_pool.h, pcnt[dc - 2])
                    s_pool.inc(nc.gpsimd.tensor_tensor(
                        tt32[2 + b][:, :], xr[:, dc * T:(dc + 1) * T],
                        mB[:, :], ALU.subtract))
                    nc.gpsimd.wait_ge(s_pool.h, s_pool.n)
                    s_pool.inc(nc.gpsimd.tensor_tensor(
                        xl[:, dc * T:(dc + 1) * T], tt32[2 + b][:, :],
                        rstd[:, :], ALU.mult))
                    pcnt[dc] = s_pool.n
            nc.vector.wait_ge(s_dve.h, s_dve.n)
            nc.vector.wait_ge(s_pool.h, s_pool.n)
        barrier()

    def w1_srcs(e, ic):
        if e < 8:
            g = w1T_d[e, :, ic * P:(ic + 1) * P]
            u = w1T_d[e, :, IEX + ic * P:IEX + (ic + 1) * P]
        else:
            q = e - 8
            g = sguT_d[:, q * IEX + ic * P:q * IEX + (ic + 1) * P]
            u = sguT_d[:, ISH + q * IEX + ic * P:ISH + q * IEX + (ic + 1) * P]
        return (g.rearrange("(dc p) j -> p dc j", p=P),
                u.rearrange("(dc p) j -> p dc j", p=P))

    def w2_src(e, dcg):
        if e < 8:
            s2 = w2T_d[e, :, dcg * 256:(dcg + 1) * 256]
        else:
            q = e - 8
            s2 = sdT_d[q * IEX:(q + 1) * IEX, dcg * 256:(dcg + 1) * 256]
        return s2.rearrange("(ic p) d -> p ic d", p=P)

    def ffn_call():
        with contextlib.ExitStack() as ph:
            _u = uid()
            psb = lambda n_, s_, d_: ph.enter_context(nc.psum_tensor(n_ + _u, s_, d_))
            lsb = lambda n_, s_, d_: ph.enter_context(nc.sbuf_tensor(n_ + _u, s_, d_))
            pb = [psb("pf%d" % i2, [P, HB], F32) for i2 in range(8)]
            w1g = [lsb("w1g%d" % i2, [P, NDC * P], BF16) for i2 in range(2)]
            w1u = [lsb("w1u%d" % i2, [P, NDC * P], BF16) for i2 in range(2)]
            w2b = [lsb("w2b%d" % i2, [P, NIC * 256], BF16) for i2 in range(4)]
            heb = lsb("heb", [P, 2 * NIC * T], BF16)
            cBb = [lsb("cB%d" % i2, [P, T], F32) for i2 in range(2)]
            sgb = [lsb("sg%d" % i2, [P, HB], BF16) for i2 in range(2)]
            tmpb = [lsb("tmp%d" % i2, [P, HB], BF16) for i2 in range(2)]
            lg9 = lsb("lg9", [P, 8 * 9], F32)
            cm9 = lsb("cm9", [P, 8 * 9], F32)
            crow = lsb("crow", [1, T], BF16)
            rsc = lsb("rsc", [P, 8 * 24], F32)

            act_r0 = s_act.n
            for tt in range(8):
                pr = pb[tt % 2]
                if tt >= 2:
                    nc.tensor.wait_ge(s_act.h, act_r0 + (tt - 1))
                for dc in range(NDC):
                    mm = nc.tensor.matmul(
                        pr[:, 0:9],
                        xl[:, dc * T + tt * P:dc * T + (tt + 1) * P],
                        gw9[:, dc * 16:dc * 16 + 9],
                        start=(dc == 0), stop=(dc == NDC - 1))
                s_pe.inc(mm)
                nc.scalar.wait_ge(s_pe.h, s_pe.n)
                s_act.inc(nc.scalar.activation(lg9[:, tt * 9:(tt + 1) * 9],
                                               pr[:, 0:9], AF.Copy))
            nc.vector.wait_ge(s_act.h, s_act.n)
            for tt in range(8):
                lg = lg9[:, tt * 9:tt * 9 + 8]
                sc = rsc[:, tt * 24:(tt + 1) * 24]
                chain(nc.vector, s_dve,
                      nc.vector.reduce_max(sc[:, 0:1], lg, axis=AX))
                chain(nc.vector, s_dve,
                      nc.vector.tensor_scalar_mul(sc[:, 1:2], sc[:, 0:1], -1.0))
                nc.scalar.wait_ge(s_dve.h, s_dve.n)
                chain(nc.scalar, s_act,
                      nc.scalar.activation(sc[:, 2:10], lg, AF.Exp,
                                           bias=sc[:, 1:2]))
                nc.vector.wait_ge(s_act.h, s_act.n)
                chain(nc.vector, s_dve,
                      nc.vector.reduce_sum(sc[:, 10:11], sc[:, 2:10], axis=AX))
                chain(nc.vector, s_dve,
                      nc.vector.reciprocal(sc[:, 11:12], sc[:, 10:11]))
                chain(nc.vector, s_dve,
                      nc.vector.tensor_scalar(sc[:, 2:10], sc[:, 2:10],
                                              sc[:, 11:12], None, ALU.mult))
                chain(nc.vector, s_dve,
                      nc.vector.reduce_max(sc[:, 12:13], sc[:, 2:10], axis=AX))
                chain(nc.vector, s_dve,
                      nc.vector.tensor_scalar(sc[:, 16:24], sc[:, 2:10],
                                              sc[:, 12:13], None, ALU.is_lt))
                chain(nc.vector, s_dve,
                      nc.vector.tensor_tensor(sc[:, 16:24], sc[:, 2:10],
                                              sc[:, 16:24], ALU.mult))
                chain(nc.vector, s_dve,
                      nc.vector.reduce_max(sc[:, 13:14], sc[:, 16:24], axis=AX))
                chain(nc.vector, s_dve,
                      nc.vector.tensor_scalar(sc[:, 16:24], sc[:, 2:10],
                                              sc[:, 13:14], None, ALU.is_ge))
                chain(nc.vector, s_dve,
                      nc.vector.tensor_tensor(sc[:, 14:15], sc[:, 12:13],
                                              sc[:, 13:14], ALU.add))
                chain(nc.vector, s_dve,
                      nc.vector.reciprocal(sc[:, 15:16], sc[:, 14:15]))
                chain(nc.vector, s_dve,
                      nc.vector.tensor_tensor(sc[:, 16:24], sc[:, 2:10],
                                              sc[:, 16:24], ALU.mult))
                chain(nc.vector, s_dve,
                      nc.vector.tensor_scalar(cm9[:, tt * 9:tt * 9 + 8],
                                              sc[:, 16:24], sc[:, 15:16], None,
                                              ALU.mult))
                nc.scalar.wait_ge(s_dve.h, s_dve.n)
                chain(nc.scalar, s_act,
                      nc.scalar.activation(cm9[:, tt * 9 + 8:tt * 9 + 9],
                                           lg9[:, tt * 9 + 8:tt * 9 + 9],
                                           AF.Sigmoid))
            nc.vector.wait_ge(s_act.h, s_act.n)

            def build_cB(e, slot, gate_dve):
                ce = e if e < 8 else 8
                nc.tensor.wait_ge(s_dve.h, gate_dve)
                nc.tensor.wait_ge(s_act.h, s_act.n)
                d_base = s_dve.n
                for tt in range(8):
                    pt = pb[6 + tt % 2]
                    if tt >= 2:
                        nc.tensor.wait_ge(s_dve.h, d_base + tt - 1)
                    tr = nc.tensor.transpose(
                        pt[0:1, 0:P], cm9[:, tt * 9 + ce:tt * 9 + ce + 1],
                        ident_f[:, :])
                    s_pe.inc(tr)
                    nc.vector.wait_ge(s_pe.h, s_pe.n)
                    s_dve.inc(nc.vector.tensor_copy(
                        crow[0:1, tt * P:(tt + 1) * P], pt[0:1, 0:P]))
                nc.tensor.wait_ge(s_dve.h, s_dve.n)
                for half in range(2):
                    mm = nc.tensor.matmul(pb[6 + half][:, :], half_b[0:1, :],
                                          crow[0:1, half * HB:(half + 1) * HB],
                                          start=True, stop=True)
                    s_pe.inc(mm)
                nc.vector.wait_ge(s_pe.h, s_pe.n)
                nc.vector.wait_ge(s_pool.h, s_pool.n)
                for half in range(2):
                    s_dve.inc(nc.vector.tensor_copy(
                        cBb[slot][:, half * HB:(half + 1) * HB],
                        pb[6 + half][:, :]))
                return s_dve.n

            w1_pe_done = {}
            silu_done = {}
            tmp_done = {}
            he_done = {}
            w2_pe_done = {}
            w2_ev_done = {}
            wo_gate = {}

            def issue_w1(e, ic, gate_pe):
                buf = (e * NIC + ic) % 2
                g_src, u_src = w1_srcs(e, ic)
                if gate_pe is not None:
                    nc.gpsimd.wait_ge(s_pe.h, gate_pe)
                i1 = nc.gpsimd.dma_start(
                    w1g[buf][:, :].rearrange("p (dc j) -> p dc j", j=P), g_src)
                sw[buf].inc(i1, 16)
                i2 = nc.gpsimd.dma_start(
                    w1u[buf][:, :].rearrange("p (dc j) -> p dc j", j=P), u_src)
                sw[buf].inc(i2, 16)

            def issue_w2(g, dcg, gate_pe):
                for s01 in range(2):
                    buf = (dcg % 2) * 2 + s01
                    if gate_pe is not None:
                        nc.gpsimd.wait_ge(s_pe.h, gate_pe)
                    i1 = nc.gpsimd.dma_start(
                        w2b[buf][:, :].rearrange("p (ic d) -> p ic d", d=256),
                        w2_src(2 * g + s01, dcg))
                    sw[2 + buf].inc(i1, 16)

            issue_w1(0, 0, None)
            issue_w1(0, 1, None)
            build_cB(0, 0, s_dve.n)
            build_cB(1, 1, s_dve.n)
            issue_w2(0, 0, None)
            issue_w2(0, 1, None)

            step = 0
            for g in range(6):
                for s01 in range(2):
                    e = 2 * g + s01
                    for ic in range(NIC):
                        gbuf = (e * NIC + ic) % 2
                        nc.tensor.wait_ge(sw[gbuf].h, sw[gbuf].n)
                        for half in range(2):
                            gps = pb[step % 2]
                            ups = pb[2 + step % 2]
                            if step >= 2:
                                nc.tensor.wait_ge(s_act.h, silu_done[step - 2])
                                nc.tensor.wait_ge(s_dve.h, tmp_done[step - 2])
                            for dc in range(NDC):
                                mm = nc.tensor.matmul(
                                    gps[:, :],
                                    w1g[gbuf][:, dc * P:(dc + 1) * P],
                                    xl[:, dc * T + half * HB:
                                       dc * T + (half + 1) * HB],
                                    start=(dc == 0), stop=(dc == NDC - 1))
                            for dc in range(NDC):
                                mm = nc.tensor.matmul(
                                    ups[:, :],
                                    w1u[gbuf][:, dc * P:(dc + 1) * P],
                                    xl[:, dc * T + half * HB:
                                       dc * T + (half + 1) * HB],
                                    start=(dc == 0), stop=(dc == NDC - 1))
                            s_pe.inc(mm)
                            w1_pe_done[(e, ic)] = s_pe.n
                            nc.scalar.wait_ge(s_pe.h, s_pe.n)
                            if step >= 2:
                                nc.scalar.wait_ge(s_dve.h, tmp_done[step - 2])
                            a = nc.scalar.activation(sgb[step % 2][:, :],
                                                     gps[:, :], AF.Silu)
                            silu_done[step] = s_act.inc(a)
                            nc.vector.wait_ge(s_act.h, silu_done[step])
                            if step >= 2:
                                nc.vector.wait_ge(s_pool.h, he_done[step - 2])
                            v = nc.vector.scalar_tensor_tensor(
                                tmpb[step % 2][:, :], sgb[step % 2][:, :], 1.0,
                                ups[:, :], ALU.mult, ALU.mult)
                            tmp_done[step] = s_dve.inc(v)
                            nc.gpsimd.wait_ge(s_dve.h, tmp_done[step])
                            hs = (s01 * NIC + ic) * T + half * HB
                            pl = nc.gpsimd.tensor_tensor(
                                heb[:, hs:hs + HB], tmpb[step % 2][:, :],
                                cBb[s01][:, half * HB:(half + 1) * HB],
                                ALU.mult)
                            he_done[step] = s_pool.inc(pl)
                            step += 1
                        nx = e * NIC + ic + 2
                        if nx < NPE * NIC:
                            ne, nic2 = divmod(nx, NIC)
                            issue_w1(ne, nic2, w1_pe_done[(e, ic)])
                last_he = he_done[step - 1]
                for dcg in range(8):
                    setb = (dcg % 2) * 4
                    nc.tensor.wait_ge(s_pool.h, last_he)
                    for s01 in range(2):
                        wb = (dcg % 2) * 2 + s01
                        nc.tensor.wait_ge(sw[2 + wb].h, sw[2 + wb].n)
                    if dcg >= 2:
                        nc.tensor.wait_ge(s_dve.h, w2_ev_done[(g, dcg - 2)])
                    first = True
                    for s01 in range(2):
                        wb = (dcg % 2) * 2 + s01
                        for ic in range(NIC):
                            for dc2 in range(2):
                                for half in range(2):
                                    mm = nc.tensor.matmul(
                                        pb[setb + dc2 * 2 + half][:, :],
                                        w2b[wb][:, ic * 256 + dc2 * P:
                                                ic * 256 + (dc2 + 1) * P],
                                        heb[:, (s01 * NIC + ic) * T + half * HB:
                                            (s01 * NIC + ic) * T
                                            + (half + 1) * HB],
                                        start=first,
                                        stop=(s01 == 1 and ic == NIC - 1))
                            first = False
                    w2_pe_done[(g, dcg)] = s_pe.inc(mm)
                    if dcg + 2 < 8:
                        issue_w2(g, dcg + 2, w2_pe_done[(g, dcg)])
                    elif g < 5:
                        issue_w2(g + 1, dcg + 2 - 8, w2_pe_done[(g, dcg)])
                    nc.vector.wait_ge(s_pe.h, w2_pe_done[(g, dcg)])
                    for dc2 in range(2):
                        for half in range(2):
                            dc = dcg * 2 + dc2
                            ev = nc.vector.tensor_tensor(
                                xr[:, dc * T + half * HB:
                                   dc * T + (half + 1) * HB],
                                xr[:, dc * T + half * HB:
                                   dc * T + (half + 1) * HB],
                                pb[setb + dc2 * 2 + half][:, :], ALU.add)
                    w2_ev_done[(g, dcg)] = s_dve.inc(ev)
                if g < 5:
                    gate = w2_ev_done[(g, 7)]
                    build_cB(2 * (g + 1), 0, gate)
                    build_cB(2 * (g + 1) + 1, 1, gate)
            nc.vector.wait_ge(s_dve.h, s_dve.n)
        barrier()

    def attn():
        with contextlib.ExitStack() as ph:
            _u = uid()
            psb = lambda n_, s_, d_: ph.enter_context(nc.psum_tensor(n_ + _u, s_, d_))
            lsb = lambda n_, s_, d_: ph.enter_context(nc.sbuf_tensor(n_ + _u, s_, d_))
            pA = psb("paA", [P, HB], F32)
            pB = psb("paB", [P, HB], F32)
            pZ = [psb("paZ%d" % i2, [P, HB], F32) for i2 in range(2)]
            pD = [psb("paD%d" % i2, [P, HB], F32) for i2 in range(2)]
            pT = [psb("paT%d" % i2, [P, HB], BF16) for i2 in range(2)]
            wkv = [lsb("wkv%d" % i2, [P, NDC * EH], BF16) for i2 in range(2)]
            wst = [lsb("wst%d" % i2, [P, NDC * EH], F32) for i2 in range(2)]
            wob = [lsb("wob%d" % i2, [P, D], BF16) for i2 in range(2)]
            kst = [lsb("kst%d" % i2, [P, T], BF16) for i2 in range(2)]
            kfull = [lsb("kfull%d" % i2, [P, 2 * T], BF16) for i2 in range(2)]
            vE = [lsb("vE%d" % i2, [P, 2 * T], BF16) for i2 in range(2)]
            vt = [lsb("vt%d" % i2, [P, 2 * T], BF16) for i2 in range(2)]
            qsb = [lsb("qsb%d" % i2, [P, T], BF16) for i2 in range(2)]
            expb = [lsb("expb%d" % i2, [P, HB], BF16) for i2 in range(3)]
            ztb = [lsb("ztb%d" % i2, [P, T], BF16) for i2 in range(2)]
            sqs = [lsb("sqs%d" % i2, [P, T], BF16) for i2 in range(2)]
            recb = [lsb("recb%d" % i2, [P, HB], F32) for i2 in range(2)]
            stat = lsb("stat", [P, 16], F32)
            qkmax = lsb("qkmax", [P, 2 * NH], F32)
            negB = lsb("negB", [P, NH], F32)
            def kv_stage(w_dram, bias_s, kv_dram, stat_col0):
                hd0 = s_hd.n
                evac_ms = {}
                square_ms = {}
                statmm_ms = {}

                def emit_square(h2):
                    if h2 >= 2:
                        nc.scalar.wait_ge(s_pe.h, statmm_ms[h2 - 2])
                    aa = nc.scalar.activation(sqs[h2 % 2][:, :],
                                              kst[h2 % 2][:, :], AF.Square)
                    square_ms[h2] = s_act.inc(aa)

                def emit_statmm(h2):
                    nc.tensor.wait_ge(s_act.h, square_ms[h2])
                    if h2 >= 1:
                        nc.tensor.wait_ge(s_dve.h, statred_ms[h2 - 1])
                    for half in range(2):
                        mm2 = nc.tensor.matmul(
                            pZ[half][:, :], ones_b[:, :],
                            sqs[h2 % 2][:, half * HB:(half + 1) * HB],
                            start=True, stop=True)
                    statmm_ms[h2] = s_pe.inc(mm2)
                    nc.vector.wait_ge(s_pe.h, statmm_ms[h2])
                    chain(nc.vector, s_dve,
                          nc.vector.reduce_max(stat[:, 1:2], pZ[0][:, :],
                                               axis=AX))
                    chain(nc.vector, s_dve,
                          nc.vector.reduce_max(stat[:, 2:3], pZ[1][:, :],
                                               axis=AX))
                    chain(nc.vector, s_dve,
                          nc.vector.tensor_tensor(
                              qkmax[:, stat_col0 + h2:stat_col0 + h2 + 1],
                              stat[:, 1:2], stat[:, 2:3], ALU.max))
                    statred_ms[h2] = s_dve.n

                statred_ms = {}
                cast_ms = {}

                def issue_w(h2):
                    wbuf2 = h2 % 2
                    if h2 >= 2:
                        nc.sync.wait_ge(s_pe.h, s_pe.n)
                    i3 = nc.sync.dma_start(
                        wst[wbuf2][:, :].rearrange("p (dc e) -> p dc e", e=EH),
                        w_dram[h2].rearrange("(dc p) e -> p dc e", p=P))
                    sw[wbuf2].inc(i3, 16)

                def cast_w(h2):
                    wbuf2 = h2 % 2
                    nc.vector.wait_ge(sw[wbuf2].h, sw[wbuf2].n)
                    if h2 >= 2:
                        nc.vector.wait_ge(s_pe.h, s_pe.n)
                    cc2 = nc.vector.tensor_copy(wkv[wbuf2][:, :],
                                                wst[wbuf2][:, :])
                    cast_ms[h2] = s_dve.inc(cc2)

                issue_w(0)
                issue_w(1)
                cast_w(0)
                for h in range(NH):
                    wbuf = h % 2
                    kb = kst[h % 2]
                    if h + 1 < NH:
                        cast_w(h + 1)
                    nc.tensor.wait_ge(s_dve.h, cast_ms[h])
                    if h >= 1:
                        nc.tensor.wait_ge(s_act.h, evac_ms[h - 1])
                    else:
                        nc.tensor.wait_ge(s_act.h, s_act.n)
                    for half in range(2):
                        pp = pA if half == 0 else pB
                        for dc in range(NDC):
                            mm = nc.tensor.matmul(
                                pp[:, :], wkv[wbuf][:, dc * EH:(dc + 1) * EH],
                                xl[:, dc * T + half * HB:
                                   dc * T + (half + 1) * HB],
                                start=(dc == 0), stop=(dc == NDC - 1))
                        s_pe.inc(mm)
                        nc.scalar.wait_ge(s_pe.h, s_pe.n)
                        if h >= 2 and half == 0:
                            nc.scalar.wait_ge(s_hd.h, hd0 + 16 * (h - 1))
                        aa = nc.scalar.activation(
                            kb[:, half * HB:(half + 1) * HB], pp[:, :],
                            AF.Identity, bias=bias_s[:, h:h + 1])
                        s_act.inc(aa)
                    evac_ms[h] = s_act.n
                    if stat_col0 is not None:
                        emit_square(h)
                    if h + 2 < NH:
                        issue_w(h + 2)
                    nc.sync.wait_ge(s_act.h, evac_ms[h])
                    s_hd.inc(nc.sync.dma_start(
                        kv_dram[h * P:(h + 1) * P, :], kb[:, :]), 16)
                    if stat_col0 is not None and h >= 1:
                        emit_statmm(h - 1)
                if stat_col0 is not None:
                    emit_statmm(NH - 1)

            kv_stage(wk_d, bk_s, kvK_in, NH)
            nc.gpsimd.wait_ge(s_hd.h, s_hd.n)
            s_cc.inc(nc.gpsimd.collective_compute(
                "AllGather", ALU.bypass, replica_groups=[list(range(8))],
                ins=[kvK_in.opt()], outs=[kvK_out.opt()]))
            nc.sync.wait_ge(s_dve.h, s_dve.n)
            s_hd.inc(nc.sync.dma_start(kstat_in, qkmax[0:1, NH:2 * NH]), 16)
            nc.gpsimd.wait_ge(s_hd.h, s_hd.n)
            s_cc.inc(nc.gpsimd.collective_compute(
                "AllReduce", ALU.max, replica_groups=[list(range(8))],
                ins=[kstat_in.opt()], outs=[kstat_out.opt()]))
            nc.scalar.wait_ge(s_hd.h, s_hd.n)
            kv_stage(wv_d, bv_s, kvV_in, None)
            nc.gpsimd.wait_ge(s_hd.h, s_hd.n)
            s_cc.inc(nc.gpsimd.collective_compute(
                "AllGather", ALU.bypass, replica_groups=[list(range(8))],
                ins=[kvV_in.opt()], outs=[kvV_out.opt()]))
            nc.scalar.wait_ge(s_hd.h, s_hd.n)
            kv_stage(wq_d, bq_s, qstage_d, 0)
            nc.sync.wait_ge(s_cc.h, 2)
            s_hd.inc(nc.sync.dma_start(qkmax[0:1, NH:2 * NH], kstat_out), 16)
            nc.vector.wait_ge(s_hd.h, s_hd.n)
            nc.vector.wait_ge(s_dve.h, s_dve.n)
            chain(nc.vector, s_dve,
                  nc.vector.tensor_tensor(stat[0:1, 0:NH], qkmax[0:1, 0:NH],
                                          qkmax[0:1, NH:2 * NH], ALU.mult))
            chain(nc.vector, s_dve,
                  nc.vector.tensor_scalar_mul(stat[0:1, 0:NH], stat[0:1, 0:NH],
                                              1.0 / 128.0))
            nc.scalar.wait_ge(s_dve.h, s_dve.n)
            chain(nc.scalar, s_act,
                  nc.scalar.activation(recb[0][0:1, 0:NH], stat[0:1, 0:NH],
                                       AF.Ln))
            chain(nc.scalar, s_act,
                  nc.scalar.activation(stat[0:1, 0:NH], recb[0][0:1, 0:NH],
                                       AF.Exp, scale=0.5))
            nc.vector.wait_ge(s_act.h, s_act.n)
            chain(nc.vector, s_dve,
                  nc.vector.tensor_scalar_mul(stat[0:1, 0:NH], stat[0:1, 0:NH],
                                              -1.0))
            nc.tensor.wait_ge(s_dve.h, s_dve.n)
            mm = nc.tensor.matmul(pA[:, 0:NH], ones_f[0:1, :],
                                  stat[0:1, 0:NH], start=True, stop=True)
            s_pe.inc(mm)
            nc.vector.wait_ge(s_pe.h, s_pe.n)
            chain(nc.vector, s_dve,
                  nc.vector.tensor_copy(negB[:, :], pA[:, 0:NH]))

            kv_done = {}
            wo_gate = {}
            pend_wo = []
            zt_done = {}
            exp_done = {}
            msk_done = {}
            avdn_done = {}
            head_pe_done = {}
            q_loaded = {}

            def issue_gathers(h):
                buf = h % 2
                nc.gpsimd.wait_ge(s_cc.h, 1)
                if h >= 2:
                    nc.gpsimd.wait_ge(s_pe.h, head_pe_done[h - 2])
                for s2 in range(2):
                    i2 = nc.gpsimd.indirect_dma_start(
                        kfull[buf][:, s2 * T:(s2 + 1) * T], None,
                        kvK_out[:, :],
                        bass.IndirectOffsetOnAxis(
                            idx_s[:, 2 * h + s2:2 * h + s2 + 1], 0))
                    s_kv.inc(i2, 16)
                nc.gpsimd.wait_ge(s_cc.h, 3)
                for s2 in range(2):
                    i2 = nc.gpsimd.indirect_dma_start(
                        vE[buf][:, s2 * T:(s2 + 1) * T], None,
                        kvV_out[:, :],
                        bass.IndirectOffsetOnAxis(
                            idx_s[:, 2 * h + s2:2 * h + s2 + 1], 0))
                    s_kv.inc(i2, 16)
                kv_done[h] = s_kv.n

            def issue_qload(h):
                buf = h % 2
                if h >= 2:
                    nc.sync.wait_ge(s_pe.h, head_pe_done[h - 2])
                i2 = nc.sync.dma_start(qsb[buf][:, :],
                                       qstage_d[h * P:(h + 1) * P, :])
                q_loaded[h] = s_hd.inc(i2, 16)

            issue_gathers(0)
            issue_gathers(1)
            issue_qload(0)
            issue_qload(1)

            for h in range(NH):
                buf = h % 2
                nc.tensor.wait_ge(s_kv.h, kv_done[h])
                if h >= 2:
                    nc.tensor.wait_ge(s_pe.h, head_pe_done[h - 2])
                act_vt0 = s_act.n
                for kt in range(16):
                    pt = pT[kt % 2]
                    if kt >= 2:
                        nc.tensor.wait_ge(s_act.h, act_vt0 + kt - 1)
                    s_pe.inc(nc.tensor.transpose(
                        pt[:, 0:P], vE[buf][:, kt * P:(kt + 1) * P],
                        ident_b[:, :]))
                    nc.scalar.wait_ge(s_pe.h, s_pe.n)
                    s_act.inc(nc.scalar.activation(
                        vt[buf][:, kt * P:(kt + 1) * P], pt[:, 0:P], AF.Copy))

                if pend_wo:
                    g2p, ztg, fn = pend_wo.pop(0)
                    fn(g2p, ztg)

                nc.tensor.wait_ge(s_hd.h, q_loaded[h])
                def emit_sc(it):
                    kt, half = divmod(it, 2)
                    psc = pA if it % 2 == 0 else pB
                    if it >= 2:
                        nc.tensor.wait_ge(s_act.h, exp_done[(h, it - 2)])
                    else:
                        nc.tensor.wait_ge(s_dve.h, s_dve.n)
                    mm = nc.tensor.matmul(
                        psc[:, :], kfull[buf][:, kt * P:(kt + 1) * P],
                        qsb[buf][:, half * HB:(half + 1) * HB],
                        start=True, stop=False)
                    g0 = 2048 - kt * P + half * HB
                    mm = nc.tensor.matmul(psc[:, :], ident_b[:, :],
                                          bigm[:, g0:g0 + HB],
                                          start=False, stop=True)
                    pe_n = s_pe.inc(mm)
                    nc.scalar.wait_ge(s_pe.h, pe_n)
                    if it >= 3:
                        nc.scalar.wait_ge(s_pe.h, avdn_done[(h, it - 3)])
                    a2 = nc.scalar.activation(expb[it % 3][:, :], psc[:, :],
                                              AF.Exp, scale=SCALE_QK,
                                              bias=negB[:, h:h + 1])
                    exp_done[(h, it)] = s_act.inc(a2)

                def emit_av(it):
                    kt, half = divmod(it, 2)
                    nc.tensor.wait_ge(s_act.h, exp_done[(h, it)])
                    mm = nc.tensor.matmul(
                        pZ[half][:, :], vt[buf][:, kt * P:(kt + 1) * P],
                        expb[it % 3][:, :],
                        start=(kt == 0), stop=(kt == 15))
                    mm2 = nc.tensor.matmul(
                        pD[half][:, :], ones_b[:, :], expb[it % 3][:, :],
                        start=(kt == 0), stop=(kt == 15))
                    avdn_done[(h, it)] = s_pe.inc(mm2)

                for it in range(32):
                    emit_sc(it)
                    if it >= 2:
                        emit_av(it - 2)
                emit_av(30)
                emit_av(31)
                head_pe_done[h] = s_pe.n
                if h + 2 < NH:
                    issue_gathers(h + 2)
                    issue_qload(h + 2)
                nc.vector.wait_ge(s_pe.h, head_pe_done[h])
                for half in range(2):
                    chain(nc.vector, s_dve,
                          nc.vector.reciprocal(recb[half][:, :],
                                               pD[half][:, :]))
                    s_dve.inc(nc.vector.tensor_tensor(
                        ztb[h % 2][:, half * HB:(half + 1) * HB],
                        pZ[half][:, :], recb[half][:, :], ALU.mult))
                zt_done[h] = s_dve.n

                if h % 2 == 0:
                    g2n = h // 2
                    for hh in range(2):
                        if g2n >= 1:
                            nc.gpsimd.wait_ge(s_pe.h, wo_gate[g2n - 1])
                        i2 = nc.gpsimd.dma_start(wob[hh][:, :],
                                                 wo_d[g2n * 2 + hh])
                        sw[4 + hh].inc(i2, 16)
                def emit_wo(g2, zt_gate):
                    nc.tensor.wait_ge(s_dve.h, zt_gate)
                    nc.tensor.wait_ge(sw[4].h, sw[4].n)
                    nc.tensor.wait_ge(sw[5].h, sw[5].n)
                    rot = [pA, pB, pZ[0], pZ[1]]
                    ev_ms = {}
                    for dc in range(NDC):
                        for half in range(2):
                            pr2 = rot[(dc * 2 + half) % 4]
                            if dc * 2 + half >= 4:
                                nc.tensor.wait_ge(s_dve.h,
                                                  ev_ms[dc * 2 + half - 4])
                            for hh in range(2):
                                mm = nc.tensor.matmul(
                                    pr2[:, :],
                                    wob[hh][:, dc * P:(dc + 1) * P],
                                    ztb[hh][:, half * HB:(half + 1) * HB],
                                    start=(hh == 0), stop=(hh == 1))
                            s_pe.inc(mm)
                            nc.vector.wait_ge(s_pe.h, s_pe.n)
                            if g2 == 0:
                                ev = nc.vector.scalar_tensor_tensor(
                                    xr[:, dc * T + half * HB:
                                       dc * T + (half + 1) * HB],
                                    pr2[:, :], bo_s[:, dc:dc + 1],
                                    xr[:, dc * T + half * HB:
                                       dc * T + (half + 1) * HB],
                                    ALU.add, ALU.add)
                            else:
                                ev = nc.vector.tensor_tensor(
                                    xr[:, dc * T + half * HB:
                                       dc * T + (half + 1) * HB],
                                    xr[:, dc * T + half * HB:
                                       dc * T + (half + 1) * HB],
                                    pr2[:, :], ALU.add)
                            ev_ms[dc * 2 + half] = s_dve.inc(ev)
                    wo_gate[g2] = s_pe.n

                if h % 2 == 1:
                    pend_wo.append((h // 2, zt_done[h], emit_wo))
            g2l, ztg, fn = pend_wo.pop(0)
            fn(g2l, ztg)
            nc.vector.wait_ge(s_dve.h, s_dve.n)
        barrier()

    layer_norm()
    ffn_call()
    if n_sub >= 2:
        layer_norm()
        attn()
    if n_sub >= 3:
        layer_norm()
        ffn_call()

    for s in sems.values():
        if s.n:
            nc.sync.wait_ge(s.h, s.n)
    out = nc.sync.dma_start(
        y_d.rearrange("(dc p) t -> p dc t", p=P),
        xr[:, :].rearrange("p (dc t) -> p dc t", t=T))
    s_out.inc(out, 16)
    nc.sync.wait_ge(s_out.h, s_out.n)
    st.close()
    return nc


def _stage_inputs(inputs):
    f32 = np.float32
    asf = lambda a: np.ascontiguousarray(np.asarray(a, dtype=f32))
    x = asf(inputs["L0_attn_x"])
    shared = {
        "w1T": np.ascontiguousarray(asf(inputs["L0_ffn_w1"]).transpose(0, 2, 1)),
        "w2T": np.ascontiguousarray(asf(inputs["L0_ffn_w2"]).transpose(0, 2, 1)),
        "sguT": np.ascontiguousarray(asf(inputs["L0_ffn_shared_gate_up_w"]).T),
        "sdT": np.ascontiguousarray(asf(inputs["L0_ffn_shared_down_w"]).T),
        "gw9T": np.ascontiguousarray(
            np.concatenate([asf(inputs["L0_ffn_gate_w"]),
                            asf(inputs["L0_ffn_shared_gate_w"]),
                            np.zeros((7, D), f32)], axis=0).T),
        "wq": asf(inputs["L0_attn_W_Q"]),
        "wk": asf(inputs["L0_attn_W_K"]),
        "wv": asf(inputs["L0_attn_W_V"]),
        "wo": asf(inputs["L0_attn_W_O"]),
        "bqT": np.ascontiguousarray(asf(inputs["L0_attn_b_Q"]).T),
        "bkT": np.ascontiguousarray(asf(inputs["L0_attn_b_K"]).T),
        "bvT": np.ascontiguousarray(asf(inputs["L0_attn_b_V"]).T),
        "bo": np.ascontiguousarray(asf(inputs["L0_attn_b_O"]).reshape(NDC, P).T),
    }
    per_core = []
    for r in range(8):
        b, hh = r // 2, r % 2
        shard = np.ascontiguousarray(x[b, hh * T:(hh + 1) * T, :].T)
        idx = np.empty((P, 2 * NH), np.int32)
        for h in range(NH):
            idx[:, 2 * h + 0] = (2 * b) * D + h * P + np.arange(P)
            idx[:, 2 * h + 1] = (2 * b + 1) * D + h * P + np.arange(P)
        m = {"xT": shard, "qbase": np.array([[hh * T]], f32), "kvidx": idx}
        m.update(shared)
        per_core.append(m)
    return per_core


def kernel(**inputs):
    per_core = _stage_inputs(inputs)
    nc = build_program(n_sub=3)
    res = run_bass_kernel_spmd(nc, per_core, core_ids=list(range(8)))
    out = np.empty((4, 2048, D), np.float32)
    for r in range(8):
        b, hh = r // 2, r % 2
        out[b, hh * T:(hh + 1) * T, :] = np.asarray(res.results[r]["y"]).T
    return out


# version 05
# speedup vs baseline: 1.0107x; 1.0107x over previous
import contextlib
import numpy as np
import concourse.bass as bass
import concourse.mybir as mybir
from concourse.bass_utils import run_bass_kernel_spmd

F32 = mybir.dt.float32
BF16 = mybir.dt.bfloat16
I32 = mybir.dt.int32
AF = mybir.ActivationFunctionType
ALU = mybir.AluOpType
AX = mybir.AxisListType.X

P = 128
D = 2048
NDC = 16
T = 1024
NH = 16
EH = 128
IEX = 1408
NIC = 11
NPE = 12
ISH = 5632
SCALE_QK = float(1.0 / np.sqrt(128.0))
NEG_BIG = -100000.0
LN_EPS = 1e-5
HB = 512


class Sem:
    def __init__(self, st, nc, name):
        self.h = st.enter_context(nc.semaphore(name))
        self.n = 0

    def inc(self, instr, k=1):
        instr.then_inc(self.h, k)
        self.n += k
        return self.n


def build_program(n_sub=3, debug=False):
    nc = bass.Bass("TRN2", target_bir_lowering=False)
    uid_ctr = [0]
    def uid():
        uid_ctr[0] += 1
        return "_u%d" % uid_ctr[0]

    xT_d = nc.dram_tensor("xT", [D, T], F32, kind="ExternalInput").ap()
    qb_d = nc.dram_tensor("qbase", [1, 1], F32, kind="ExternalInput").ap()
    idx_d = nc.dram_tensor("kvidx", [P, 2 * NH], I32, kind="ExternalInput").ap()
    w1gp_d = nc.dram_tensor("w1gp", [NPE * NIC * P, NDC * P], F32,
                            kind="ExternalInput").ap()
    w1up_d = nc.dram_tensor("w1up", [NPE * NIC * P, NDC * P], F32,
                            kind="ExternalInput").ap()
    w2p_d = nc.dram_tensor("w2p", [NPE * 8 * P, NIC * 256], F32,
                           kind="ExternalInput").ap()
    gw9T_d = nc.dram_tensor("gw9T", [D, 16], F32, kind="ExternalInput").ap()
    wq_d = nc.dram_tensor("wqp", [NH * P, NDC * EH], F32, kind="ExternalInput").ap()
    wk_d = nc.dram_tensor("wkp", [NH * P, NDC * EH], F32, kind="ExternalInput").ap()
    wv_d = nc.dram_tensor("wvp", [NH * P, NDC * EH], F32, kind="ExternalInput").ap()
    wo_d = nc.dram_tensor("wo", [NH, EH, D], F32, kind="ExternalInput").ap()
    bqT_d = nc.dram_tensor("bqT", [P, NH], F32, kind="ExternalInput").ap()
    bkT_d = nc.dram_tensor("bkT", [P, NH], F32, kind="ExternalInput").ap()
    bvT_d = nc.dram_tensor("bvT", [P, NH], F32, kind="ExternalInput").ap()
    bo_d = nc.dram_tensor("bo", [P, NDC], F32, kind="ExternalInput").ap()
    y_d = nc.dram_tensor("y", [D, T], F32, kind="ExternalOutput").ap()
    dbg_d = (nc.dram_tensor("dbg", [P, 4096], F32, kind="ExternalOutput").ap()
             if debug else None)
    dbg2_d = (nc.dram_tensor("dbg2", [NH * P, T], F32, kind="ExternalOutput").ap()
              if debug else None)

    kvK_in = nc.dram_tensor("kvK_in", [D, T], BF16).ap()
    kvV_in = nc.dram_tensor("kvV_in", [D, T], BF16).ap()
    kvK_out = nc.dram_tensor("kvK_out", [8 * D, T], BF16).ap()
    kvV_out = nc.dram_tensor("kvV_out", [8 * D, T], BF16).ap()
    qstage_d = nc.dram_tensor("qstage", [D, T], BF16).ap()
    kstat_in = nc.dram_tensor("kstat_in", [1, NH], F32).ap()
    kstat_out = nc.dram_tensor("kstat_out", [1, NH], F32).ap()

    st = contextlib.ExitStack()
    sb = lambda name, shape, dt: st.enter_context(nc.sbuf_tensor(name, shape, dt))

    xr = sb("xr", [P, NDC * T], F32)
    xl = sb("xl", [P, NDC * T], BF16)
    ones_b = sb("ones_b", [P, P], BF16)
    ones_f = sb("ones_f", [1, P], F32)
    half_f = sb("half_f", [1, P], F32)
    half_b = sb("half_b", [1, P], BF16)
    ident_f = sb("ident_f", [P, P], F32)
    ident_b = sb("ident_b", [P, P], BF16)
    iota_p = sb("iota_p", [P, 1], F32)
    qbB = sb("qbB", [P, 1], F32)
    gw9 = sb("gw9", [P, NDC * 16], BF16)
    bq_s = sb("bq_s", [P, NH], F32)
    bk_s = sb("bk_s", [P, NH], F32)
    bv_s = sb("bv_s", [P, NH], F32)
    bo_s = sb("bo_s", [P, NDC], F32)
    qb_s = sb("qb_s", [1, 1], F32)
    idx_s = sb("idx_s", [P, 2 * NH], I32)
    iota_sml = sb("iota_sml", [P, P], F32)
    bigm = sb("bigm", [P, 3072], BF16)

    sems = {}
    for name in ["pe", "dve", "act", "pool", "hd", "cc", "out", "kv",
                 "w0", "w1", "w2", "w3", "w4", "w5"]:
        sems[name] = Sem(st, nc, "s_" + name)
    s_pe, s_dve, s_act = sems["pe"], sems["dve"], sems["act"]
    s_pool, s_hd, s_cc = sems["pool"], sems["hd"], sems["cc"]
    s_out, s_kv = sems["out"], sems["kv"]
    sw = [sems["w%d" % i] for i in range(6)]

    ENGS = [nc.tensor, nc.vector, nc.scalar, nc.gpsimd, nc.sync]

    def barrier():
        for eng in ENGS:
            for s in sems.values():
                if s.n:
                    eng.wait_ge(s.h, s.n)

    def chain(eng, sem, instr, k=1):
        sem.inc(instr, k)
        eng.wait_ge(sem.h, sem.n)

    i = nc.sync.dma_start(
        xr[:, :].rearrange("p (dc t) -> p dc t", t=T),
        xT_d.rearrange("(dc p) t -> p dc t", p=P))
    s_hd.inc(i, 16)
    for dst, src in [(qb_s[:, :], qb_d), (idx_s[:, :], idx_d),
                     (bq_s[:, :], bqT_d), (bk_s[:, :], bkT_d),
                     (bv_s[:, :], bvT_d)]:
        s_hd.inc(nc.sync.dma_start(dst, src), 16)
    s_pool.inc(nc.gpsimd.dma_start(bo_s[:, :], bo_d), 16)
    s_pool.inc(nc.gpsimd.dma_start(
        gw9[:, :].rearrange("p (dc w) -> p dc w", w=16),
        gw9T_d.rearrange("(dc p) w -> p dc w", p=P)), 16)
    chain(nc.gpsimd, s_pool,
          nc.gpsimd.iota(iota_p[:, :], pattern=[[0, 1]], base=0,
                         channel_multiplier=1,
                         allow_small_or_imprecise_dtypes=True))
    chain(nc.gpsimd, s_pool,
          nc.gpsimd.iota(iota_sml[:, :], pattern=[[1, P]], base=0,
                         channel_multiplier=0,
                         allow_small_or_imprecise_dtypes=True))
    nc.vector.memset(ones_b[:, :], 1.0)
    nc.vector.memset(ones_f[:, :], 1.0)
    nc.vector.memset(half_f[:, :], 0.5)
    nc.vector.memset(half_b[:, :], 0.5)
    nc.vector.wait_ge(s_pool.h, s_pool.n)
    chain(nc.vector, s_dve,
          nc.vector.tensor_scalar(ident_f[:, :], iota_sml[:, :], iota_p[:, 0:1],
                                  None, ALU.is_equal))
    chain(nc.vector, s_dve, nc.vector.tensor_copy(ident_b[:, :], ident_f[:, :]))
    with nc.psum_tensor("pqb2", [P, 512], F32) as pqb:
        nc.tensor.wait_ge(s_hd.h, s_hd.n)
        nc.tensor.wait_ge(s_dve.h, s_dve.n)
        mm = nc.tensor.matmul(pqb[:, 0:1], ones_f[0:1, :], qb_s[0:1, 0:1],
                              start=True, stop=True)
        s_pe.inc(mm)
        nc.vector.wait_ge(s_pe.h, s_pe.n)
        chain(nc.vector, s_dve, nc.vector.tensor_copy(qbB[:, :], pqb[:, 0:1]))
    with nc.sbuf_tensor("iota3k_tmp", [P, 3072], F32) as iota3k, \
         nc.sbuf_tensor("tcol_tmp", [P, 1], F32) as tcol:
        chain(nc.gpsimd, s_pool,
              nc.gpsimd.iota(iota3k[:, :], pattern=[[1, 3072]], base=-2048,
                             channel_multiplier=0,
                             allow_small_or_imprecise_dtypes=True))
        nc.vector.wait_ge(s_pool.h, s_pool.n)
        nc.vector.wait_ge(s_dve.h, s_dve.n)
        nc.vector.wait_ge(s_pe.h, s_pe.n)
        chain(nc.vector, s_dve,
              nc.vector.tensor_tensor(tcol[:, :], iota_p[:, :], qbB[:, :],
                                      ALU.subtract))
        chain(nc.vector, s_dve,
              nc.vector.tensor_scalar(bigm[:, :], iota3k[:, :], tcol[:, 0:1],
                                      NEG_BIG, ALU.is_lt, ALU.mult))
    barrier()

    def layer_norm():
        with contextlib.ExitStack() as ph:
            _u = uid()
            psb = lambda n_, s_, d_: ph.enter_context(nc.psum_tensor(n_ + _u, s_, d_))
            lsb = lambda n_, s_, d_: ph.enter_context(nc.sbuf_tensor(n_ + _u, s_, d_))
            p_s = [psb("pls%d" % i2, [P, HB], F32) for i2 in range(2)]
            p_q = [psb("plq%d" % i2, [P, HB], F32) for i2 in range(2)]
            xb_t = [lsb("lnxb%d" % i2, [P, T], BF16) for i2 in range(2)]
            sq_t = [lsb("lnsq%d" % i2, [P, T], BF16) for i2 in range(2)]
            mB = lsb("lnmB", [P, T], F32)
            v1 = lsb("lnv1", [P, T], F32)
            rstd = lsb("lnrstd", [P, T], F32)
            tt32 = [lsb("lnt%d" % i2, [P, T], F32) for i2 in range(4)]

            dve0, act0, pe0 = s_dve.n, s_act.n, s_pe.n
            for dc in range(NDC):
                b = dc % 2
                if dc >= 2:
                    nc.vector.wait_ge(s_pe.h, pe0 + 4 * (dc - 1))
                    nc.scalar.wait_ge(s_pe.h, pe0 + 4 * (dc - 1))
                s_dve.inc(nc.vector.tensor_copy(xb_t[b][:, :],
                                                xr[:, dc * T:(dc + 1) * T]))
                s_act.inc(nc.scalar.activation(sq_t[b][:, :],
                                               xr[:, dc * T:(dc + 1) * T],
                                               AF.Square))
                nc.tensor.wait_ge(s_dve.h, dve0 + dc + 1)
                nc.tensor.wait_ge(s_act.h, act0 + dc + 1)
                for half in range(2):
                    s_pe.inc(nc.tensor.matmul(
                        p_s[half][:, :], ones_b[:, :],
                        xb_t[b][:, half * HB:(half + 1) * HB],
                        start=(dc == 0), stop=(dc == NDC - 1)))
                    s_pe.inc(nc.tensor.matmul(
                        p_q[half][:, :], ones_b[:, :],
                        sq_t[b][:, half * HB:(half + 1) * HB],
                        start=(dc == 0), stop=(dc == NDC - 1)))
            nc.scalar.wait_ge(s_pe.h, s_pe.n)
            for half in range(2):
                sl = slice(half * HB, (half + 1) * HB)
                s_act.inc(nc.scalar.activation(mB[:, sl], p_s[half][:, :],
                                               AF.Copy, scale=1.0 / D))
                s_act.inc(nc.scalar.activation(v1[:, sl], p_q[half][:, :],
                                               AF.Copy, scale=1.0 / D))
            nc.vector.wait_ge(s_act.h, s_act.n)
            chain(nc.vector, s_dve,
                  nc.vector.tensor_tensor(tt32[0][:, :], mB[:, :], mB[:, :],
                                          ALU.mult))
            chain(nc.vector, s_dve,
                  nc.vector.tensor_tensor(v1[:, :], v1[:, :], tt32[0][:, :],
                                          ALU.subtract))
            chain(nc.vector, s_dve,
                  nc.vector.tensor_scalar_add(v1[:, :], v1[:, :], LN_EPS))
            nc.scalar.wait_ge(s_dve.h, s_dve.n)
            chain(nc.scalar, s_act,
                  nc.scalar.activation(tt32[1][:, :], v1[:, :], AF.Ln))
            chain(nc.scalar, s_act,
                  nc.scalar.activation(rstd[:, :], tt32[1][:, :], AF.Exp,
                                       scale=-0.5))
            nc.vector.wait_ge(s_act.h, s_act.n)
            nc.gpsimd.wait_ge(s_act.h, s_act.n)
            dcnt = {}
            pcnt = {}
            for dc in range(NDC):
                b = dc % 2
                if dc < 10:
                    if dc >= 2:
                        nc.vector.wait_ge(s_dve.h, dcnt[dc - 2])
                    s_dve.inc(nc.vector.tensor_tensor(
                        tt32[b][:, :], xr[:, dc * T:(dc + 1) * T],
                        mB[:, :], ALU.subtract))
                    nc.vector.wait_ge(s_dve.h, s_dve.n)
                    s_dve.inc(nc.vector.tensor_tensor(
                        xl[:, dc * T:(dc + 1) * T], tt32[b][:, :],
                        rstd[:, :], ALU.mult))
                    dcnt[dc] = s_dve.n
                else:
                    if dc >= 12:
                        nc.gpsimd.wait_ge(s_pool.h, pcnt[dc - 2])
                    s_pool.inc(nc.gpsimd.tensor_tensor(
                        tt32[2 + b][:, :], xr[:, dc * T:(dc + 1) * T],
                        mB[:, :], ALU.subtract))
                    nc.gpsimd.wait_ge(s_pool.h, s_pool.n)
                    s_pool.inc(nc.gpsimd.tensor_tensor(
                        xl[:, dc * T:(dc + 1) * T], tt32[2 + b][:, :],
                        rstd[:, :], ALU.mult))
                    pcnt[dc] = s_pool.n
            nc.vector.wait_ge(s_dve.h, s_dve.n)
            nc.vector.wait_ge(s_pool.h, s_pool.n)
        barrier()

    def w1_srcs(e, ic):
        r0 = (e * NIC + ic) * P
        return w1gp_d[r0:r0 + P, :], w1up_d[r0:r0 + P, :]

    def w2_src(e, dcg):
        r0 = (e * 8 + dcg) * P
        return w2p_d[r0:r0 + P, :]

    def ffn_call():
        with contextlib.ExitStack() as ph:
            _u = uid()
            psb = lambda n_, s_, d_: ph.enter_context(nc.psum_tensor(n_ + _u, s_, d_))
            lsb = lambda n_, s_, d_: ph.enter_context(nc.sbuf_tensor(n_ + _u, s_, d_))
            pb = [psb("pf%d" % i2, [P, HB], F32) for i2 in range(8)]
            w1g = [lsb("w1g%d" % i2, [P, NDC * P], BF16) for i2 in range(2)]
            w1u = [lsb("w1u%d" % i2, [P, NDC * P], BF16) for i2 in range(2)]
            w2b = [lsb("w2b%d" % i2, [P, NIC * 256], BF16) for i2 in range(4)]
            heb = lsb("heb", [P, 2 * NIC * T], BF16)
            cBb = [lsb("cB%d" % i2, [P, T], F32) for i2 in range(2)]
            sgb = [lsb("sg%d" % i2, [P, HB], BF16) for i2 in range(2)]
            tmpb = [lsb("tmp%d" % i2, [P, HB], BF16) for i2 in range(2)]
            lg9 = lsb("lg9", [P, 8 * 9], F32)
            cm9 = lsb("cm9", [P, 8 * 9], F32)
            crow = lsb("crow", [1, T], BF16)
            rsc = lsb("rsc", [P, 8 * 24], F32)

            act_r0 = s_act.n
            for tt in range(8):
                pr = pb[tt % 2]
                if tt >= 2:
                    nc.tensor.wait_ge(s_act.h, act_r0 + (tt - 1))
                for dc in range(NDC):
                    mm = nc.tensor.matmul(
                        pr[:, 0:9],
                        xl[:, dc * T + tt * P:dc * T + (tt + 1) * P],
                        gw9[:, dc * 16:dc * 16 + 9],
                        start=(dc == 0), stop=(dc == NDC - 1))
                s_pe.inc(mm)
                nc.scalar.wait_ge(s_pe.h, s_pe.n)
                s_act.inc(nc.scalar.activation(lg9[:, tt * 9:(tt + 1) * 9],
                                               pr[:, 0:9], AF.Copy))
            nc.vector.wait_ge(s_act.h, s_act.n)
            for tt in range(8):
                lg = lg9[:, tt * 9:tt * 9 + 8]
                sc = rsc[:, tt * 24:(tt + 1) * 24]
                chain(nc.vector, s_dve,
                      nc.vector.reduce_max(sc[:, 0:1], lg, axis=AX))
                chain(nc.vector, s_dve,
                      nc.vector.tensor_scalar_mul(sc[:, 1:2], sc[:, 0:1], -1.0))
                nc.scalar.wait_ge(s_dve.h, s_dve.n)
                chain(nc.scalar, s_act,
                      nc.scalar.activation(sc[:, 2:10], lg, AF.Exp,
                                           bias=sc[:, 1:2]))
                nc.vector.wait_ge(s_act.h, s_act.n)
                chain(nc.vector, s_dve,
                      nc.vector.reduce_sum(sc[:, 10:11], sc[:, 2:10], axis=AX))
                chain(nc.vector, s_dve,
                      nc.vector.reciprocal(sc[:, 11:12], sc[:, 10:11]))
                chain(nc.vector, s_dve,
                      nc.vector.tensor_scalar(sc[:, 2:10], sc[:, 2:10],
                                              sc[:, 11:12], None, ALU.mult))
                chain(nc.vector, s_dve,
                      nc.vector.reduce_max(sc[:, 12:13], sc[:, 2:10], axis=AX))
                chain(nc.vector, s_dve,
                      nc.vector.tensor_scalar(sc[:, 16:24], sc[:, 2:10],
                                              sc[:, 12:13], None, ALU.is_lt))
                chain(nc.vector, s_dve,
                      nc.vector.tensor_tensor(sc[:, 16:24], sc[:, 2:10],
                                              sc[:, 16:24], ALU.mult))
                chain(nc.vector, s_dve,
                      nc.vector.reduce_max(sc[:, 13:14], sc[:, 16:24], axis=AX))
                chain(nc.vector, s_dve,
                      nc.vector.tensor_scalar(sc[:, 16:24], sc[:, 2:10],
                                              sc[:, 13:14], None, ALU.is_ge))
                chain(nc.vector, s_dve,
                      nc.vector.tensor_tensor(sc[:, 14:15], sc[:, 12:13],
                                              sc[:, 13:14], ALU.add))
                chain(nc.vector, s_dve,
                      nc.vector.reciprocal(sc[:, 15:16], sc[:, 14:15]))
                chain(nc.vector, s_dve,
                      nc.vector.tensor_tensor(sc[:, 16:24], sc[:, 2:10],
                                              sc[:, 16:24], ALU.mult))
                chain(nc.vector, s_dve,
                      nc.vector.tensor_scalar(cm9[:, tt * 9:tt * 9 + 8],
                                              sc[:, 16:24], sc[:, 15:16], None,
                                              ALU.mult))
                nc.scalar.wait_ge(s_dve.h, s_dve.n)
                chain(nc.scalar, s_act,
                      nc.scalar.activation(cm9[:, tt * 9 + 8:tt * 9 + 9],
                                           lg9[:, tt * 9 + 8:tt * 9 + 9],
                                           AF.Sigmoid))
            nc.vector.wait_ge(s_act.h, s_act.n)

            def build_cB(e, slot, gate_dve):
                ce = e if e < 8 else 8
                nc.tensor.wait_ge(s_dve.h, gate_dve)
                nc.tensor.wait_ge(s_act.h, s_act.n)
                d_base = s_dve.n
                for tt in range(8):
                    pt = pb[6 + tt % 2]
                    if tt >= 2:
                        nc.tensor.wait_ge(s_dve.h, d_base + tt - 1)
                    tr = nc.tensor.transpose(
                        pt[0:1, 0:P], cm9[:, tt * 9 + ce:tt * 9 + ce + 1],
                        ident_f[:, :])
                    s_pe.inc(tr)
                    nc.vector.wait_ge(s_pe.h, s_pe.n)
                    s_dve.inc(nc.vector.tensor_copy(
                        crow[0:1, tt * P:(tt + 1) * P], pt[0:1, 0:P]))
                nc.tensor.wait_ge(s_dve.h, s_dve.n)
                for half in range(2):
                    mm = nc.tensor.matmul(pb[6 + half][:, :], half_b[0:1, :],
                                          crow[0:1, half * HB:(half + 1) * HB],
                                          start=True, stop=True)
                    s_pe.inc(mm)
                nc.vector.wait_ge(s_pe.h, s_pe.n)
                nc.vector.wait_ge(s_pool.h, s_pool.n)
                for half in range(2):
                    s_dve.inc(nc.vector.tensor_copy(
                        cBb[slot][:, half * HB:(half + 1) * HB],
                        pb[6 + half][:, :]))
                return s_dve.n

            w1_pe_done = {}
            silu_done = {}
            tmp_done = {}
            he_done = {}
            w2_pe_done = {}
            w2_ev_done = {}
            wo_gate = {}

            def issue_w1(e, ic, gate_pe):
                buf = (e * NIC + ic) % 2
                g_src, u_src = w1_srcs(e, ic)
                if gate_pe is not None:
                    nc.gpsimd.wait_ge(s_pe.h, gate_pe)
                i1 = nc.gpsimd.dma_start(w1g[buf][:, :], g_src)
                sw[buf].inc(i1, 16)
                i2 = nc.gpsimd.dma_start(w1u[buf][:, :], u_src)
                sw[buf].inc(i2, 16)

            def issue_w2(g, dcg, gate_pe):
                for s01 in range(2):
                    buf = (dcg % 2) * 2 + s01
                    if gate_pe is not None:
                        nc.gpsimd.wait_ge(s_pe.h, gate_pe)
                    i1 = nc.gpsimd.dma_start(w2b[buf][:, :],
                                             w2_src(2 * g + s01, dcg))
                    sw[2 + buf].inc(i1, 16)

            issue_w1(0, 0, None)
            issue_w1(0, 1, None)
            build_cB(0, 0, s_dve.n)
            build_cB(1, 1, s_dve.n)
            issue_w2(0, 0, None)
            issue_w2(0, 1, None)

            step = 0
            for g in range(6):
                for s01 in range(2):
                    e = 2 * g + s01
                    for ic in range(NIC):
                        gbuf = (e * NIC + ic) % 2
                        nc.tensor.wait_ge(sw[gbuf].h, sw[gbuf].n)
                        for half in range(2):
                            gps = pb[step % 2]
                            ups = pb[2 + step % 2]
                            if step >= 2:
                                nc.tensor.wait_ge(s_act.h, silu_done[step - 2])
                                nc.tensor.wait_ge(s_dve.h, tmp_done[step - 2])
                            for dc in range(NDC):
                                mm = nc.tensor.matmul(
                                    gps[:, :],
                                    w1g[gbuf][:, dc * P:(dc + 1) * P],
                                    xl[:, dc * T + half * HB:
                                       dc * T + (half + 1) * HB],
                                    start=(dc == 0), stop=(dc == NDC - 1))
                            for dc in range(NDC):
                                mm = nc.tensor.matmul(
                                    ups[:, :],
                                    w1u[gbuf][:, dc * P:(dc + 1) * P],
                                    xl[:, dc * T + half * HB:
                                       dc * T + (half + 1) * HB],
                                    start=(dc == 0), stop=(dc == NDC - 1))
                            s_pe.inc(mm)
                            w1_pe_done[(e, ic)] = s_pe.n
                            nc.scalar.wait_ge(s_pe.h, s_pe.n)
                            if step >= 2:
                                nc.scalar.wait_ge(s_dve.h, tmp_done[step - 2])
                            a = nc.scalar.activation(sgb[step % 2][:, :],
                                                     gps[:, :], AF.Silu)
                            silu_done[step] = s_act.inc(a)
                            nc.vector.wait_ge(s_act.h, silu_done[step])
                            if step >= 2:
                                nc.vector.wait_ge(s_pool.h, he_done[step - 2])
                            v = nc.vector.scalar_tensor_tensor(
                                tmpb[step % 2][:, :], sgb[step % 2][:, :], 1.0,
                                ups[:, :], ALU.mult, ALU.mult)
                            tmp_done[step] = s_dve.inc(v)
                            nc.gpsimd.wait_ge(s_dve.h, tmp_done[step])
                            hs = (s01 * NIC + ic) * T + half * HB
                            pl = nc.gpsimd.tensor_tensor(
                                heb[:, hs:hs + HB], tmpb[step % 2][:, :],
                                cBb[s01][:, half * HB:(half + 1) * HB],
                                ALU.mult)
                            he_done[step] = s_pool.inc(pl)
                            step += 1
                        nx = e * NIC + ic + 2
                        if nx < NPE * NIC:
                            ne, nic2 = divmod(nx, NIC)
                            issue_w1(ne, nic2, w1_pe_done[(e, ic)])
                last_he = he_done[step - 1]
                for dcg in range(8):
                    setb = (dcg % 2) * 4
                    nc.tensor.wait_ge(s_pool.h, last_he)
                    for s01 in range(2):
                        wb = (dcg % 2) * 2 + s01
                        nc.tensor.wait_ge(sw[2 + wb].h, sw[2 + wb].n)
                    if dcg >= 2:
                        nc.tensor.wait_ge(s_dve.h, w2_ev_done[(g, dcg - 2)])
                    first = True
                    for s01 in range(2):
                        wb = (dcg % 2) * 2 + s01
                        for ic in range(NIC):
                            for dc2 in range(2):
                                for half in range(2):
                                    mm = nc.tensor.matmul(
                                        pb[setb + dc2 * 2 + half][:, :],
                                        w2b[wb][:, ic * 256 + dc2 * P:
                                                ic * 256 + (dc2 + 1) * P],
                                        heb[:, (s01 * NIC + ic) * T + half * HB:
                                            (s01 * NIC + ic) * T
                                            + (half + 1) * HB],
                                        start=first,
                                        stop=(s01 == 1 and ic == NIC - 1))
                            first = False
                    w2_pe_done[(g, dcg)] = s_pe.inc(mm)
                    if dcg + 2 < 8:
                        issue_w2(g, dcg + 2, w2_pe_done[(g, dcg)])
                    elif g < 5:
                        issue_w2(g + 1, dcg + 2 - 8, w2_pe_done[(g, dcg)])
                    nc.vector.wait_ge(s_pe.h, w2_pe_done[(g, dcg)])
                    for dc2 in range(2):
                        for half in range(2):
                            dc = dcg * 2 + dc2
                            ev = nc.vector.tensor_tensor(
                                xr[:, dc * T + half * HB:
                                   dc * T + (half + 1) * HB],
                                xr[:, dc * T + half * HB:
                                   dc * T + (half + 1) * HB],
                                pb[setb + dc2 * 2 + half][:, :], ALU.add)
                    w2_ev_done[(g, dcg)] = s_dve.inc(ev)
                if g < 5:
                    gate = w2_ev_done[(g, 7)]
                    build_cB(2 * (g + 1), 0, gate)
                    build_cB(2 * (g + 1) + 1, 1, gate)
            nc.vector.wait_ge(s_dve.h, s_dve.n)
        barrier()

    def attn():
        with contextlib.ExitStack() as ph:
            _u = uid()
            psb = lambda n_, s_, d_: ph.enter_context(nc.psum_tensor(n_ + _u, s_, d_))
            lsb = lambda n_, s_, d_: ph.enter_context(nc.sbuf_tensor(n_ + _u, s_, d_))
            pA = psb("paA", [P, HB], F32)
            pB = psb("paB", [P, HB], F32)
            pZ = [psb("paZ%d" % i2, [P, HB], F32) for i2 in range(2)]
            pD = [psb("paD%d" % i2, [P, HB], F32) for i2 in range(2)]
            pT = [psb("paT%d" % i2, [P, HB], BF16) for i2 in range(2)]
            wkv = [lsb("wkv%d" % i2, [P, NDC * EH], BF16) for i2 in range(2)]
            wst = [lsb("wst%d" % i2, [P, NDC * EH], F32) for i2 in range(2)]
            wob = [lsb("wob%d" % i2, [P, D], BF16) for i2 in range(2)]
            kst = [lsb("kst%d" % i2, [P, T], BF16) for i2 in range(2)]
            kfull = [lsb("kfull%d" % i2, [P, 2 * T], BF16) for i2 in range(2)]
            vE = [lsb("vE%d" % i2, [P, 2 * T], BF16) for i2 in range(2)]
            vt = [lsb("vt%d" % i2, [P, 2 * T], BF16) for i2 in range(2)]
            qsb = [lsb("qsb%d" % i2, [P, T], BF16) for i2 in range(2)]
            expb = [lsb("expb%d" % i2, [P, HB], BF16) for i2 in range(3)]
            ztb = [lsb("ztb%d" % i2, [P, T], BF16) for i2 in range(2)]
            sqs = [lsb("sqs%d" % i2, [P, T], BF16) for i2 in range(2)]
            recb = [lsb("recb%d" % i2, [P, HB], F32) for i2 in range(2)]
            stat = lsb("stat", [P, 16], F32)
            qkmax = lsb("qkmax", [P, 2 * NH], F32)
            negB = lsb("negB", [P, NH], F32)
            def kv_stage(w_dram, bias_s, kv_dram, stat_col0):
                hd0 = s_hd.n
                evac_ms = {}
                square_ms = {}
                statmm_ms = {}

                def emit_square(h2):
                    if h2 >= 2:
                        nc.scalar.wait_ge(s_pe.h, statmm_ms[h2 - 2])
                    aa = nc.scalar.activation(sqs[h2 % 2][:, :],
                                              kst[h2 % 2][:, :], AF.Square)
                    square_ms[h2] = s_act.inc(aa)

                def emit_statmm(h2):
                    nc.tensor.wait_ge(s_act.h, square_ms[h2])
                    if h2 >= 1:
                        nc.tensor.wait_ge(s_dve.h, statred_ms[h2 - 1])
                    for half in range(2):
                        mm2 = nc.tensor.matmul(
                            pZ[half][:, :], ones_b[:, :],
                            sqs[h2 % 2][:, half * HB:(half + 1) * HB],
                            start=True, stop=True)
                    statmm_ms[h2] = s_pe.inc(mm2)
                    nc.vector.wait_ge(s_pe.h, statmm_ms[h2])
                    chain(nc.vector, s_dve,
                          nc.vector.reduce_max(stat[:, 1:2], pZ[0][:, :],
                                               axis=AX))
                    chain(nc.vector, s_dve,
                          nc.vector.reduce_max(stat[:, 2:3], pZ[1][:, :],
                                               axis=AX))
                    chain(nc.vector, s_dve,
                          nc.vector.tensor_tensor(
                              qkmax[:, stat_col0 + h2:stat_col0 + h2 + 1],
                              stat[:, 1:2], stat[:, 2:3], ALU.max))
                    statred_ms[h2] = s_dve.n

                statred_ms = {}
                cast_ms = {}

                def issue_w(h2):
                    wbuf2 = h2 % 2
                    if h2 >= 2:
                        nc.sync.wait_ge(s_pe.h, s_pe.n)
                    i3 = nc.sync.dma_start(wst[wbuf2][:, :],
                                           w_dram[h2 * P:(h2 + 1) * P, :])
                    sw[wbuf2].inc(i3, 16)

                def cast_w(h2):
                    wbuf2 = h2 % 2
                    nc.vector.wait_ge(sw[wbuf2].h, sw[wbuf2].n)
                    if h2 >= 2:
                        nc.vector.wait_ge(s_pe.h, s_pe.n)
                    cc2 = nc.vector.tensor_copy(wkv[wbuf2][:, :],
                                                wst[wbuf2][:, :])
                    cast_ms[h2] = s_dve.inc(cc2)

                issue_w(0)
                issue_w(1)
                cast_w(0)
                for h in range(NH):
                    wbuf = h % 2
                    kb = kst[h % 2]
                    if h + 1 < NH:
                        cast_w(h + 1)
                    nc.tensor.wait_ge(s_dve.h, cast_ms[h])
                    if h >= 1:
                        nc.tensor.wait_ge(s_act.h, evac_ms[h - 1])
                    else:
                        nc.tensor.wait_ge(s_act.h, s_act.n)
                    for half in range(2):
                        pp = pA if half == 0 else pB
                        for dc in range(NDC):
                            mm = nc.tensor.matmul(
                                pp[:, :], wkv[wbuf][:, dc * EH:(dc + 1) * EH],
                                xl[:, dc * T + half * HB:
                                   dc * T + (half + 1) * HB],
                                start=(dc == 0), stop=(dc == NDC - 1))
                        s_pe.inc(mm)
                        nc.scalar.wait_ge(s_pe.h, s_pe.n)
                        if h >= 2 and half == 0:
                            nc.scalar.wait_ge(s_hd.h, hd0 + 16 * (h - 1))
                        aa = nc.scalar.activation(
                            kb[:, half * HB:(half + 1) * HB], pp[:, :],
                            AF.Identity, bias=bias_s[:, h:h + 1])
                        s_act.inc(aa)
                    evac_ms[h] = s_act.n
                    if stat_col0 is not None:
                        emit_square(h)
                    if h + 2 < NH:
                        issue_w(h + 2)
                    nc.sync.wait_ge(s_act.h, evac_ms[h])
                    s_hd.inc(nc.sync.dma_start(
                        kv_dram[h * P:(h + 1) * P, :], kb[:, :]), 16)
                    if stat_col0 is not None and h >= 1:
                        emit_statmm(h - 1)
                if stat_col0 is not None:
                    emit_statmm(NH - 1)

            kv_stage(wk_d, bk_s, kvK_in, NH)
            nc.gpsimd.wait_ge(s_hd.h, s_hd.n)
            s_cc.inc(nc.gpsimd.collective_compute(
                "AllGather", ALU.bypass, replica_groups=[list(range(8))],
                ins=[kvK_in.opt()], outs=[kvK_out.opt()]))
            nc.sync.wait_ge(s_dve.h, s_dve.n)
            s_hd.inc(nc.sync.dma_start(kstat_in, qkmax[0:1, NH:2 * NH]), 16)
            nc.gpsimd.wait_ge(s_hd.h, s_hd.n)
            s_cc.inc(nc.gpsimd.collective_compute(
                "AllReduce", ALU.max, replica_groups=[list(range(8))],
                ins=[kstat_in.opt()], outs=[kstat_out.opt()]))
            nc.scalar.wait_ge(s_hd.h, s_hd.n)
            kv_stage(wv_d, bv_s, kvV_in, None)
            nc.gpsimd.wait_ge(s_hd.h, s_hd.n)
            s_cc.inc(nc.gpsimd.collective_compute(
                "AllGather", ALU.bypass, replica_groups=[list(range(8))],
                ins=[kvV_in.opt()], outs=[kvV_out.opt()]))
            nc.scalar.wait_ge(s_hd.h, s_hd.n)
            kv_stage(wq_d, bq_s, qstage_d, 0)
            nc.sync.wait_ge(s_cc.h, 2)
            s_hd.inc(nc.sync.dma_start(qkmax[0:1, NH:2 * NH], kstat_out), 16)
            nc.vector.wait_ge(s_hd.h, s_hd.n)
            nc.vector.wait_ge(s_dve.h, s_dve.n)
            chain(nc.vector, s_dve,
                  nc.vector.tensor_tensor(stat[0:1, 0:NH], qkmax[0:1, 0:NH],
                                          qkmax[0:1, NH:2 * NH], ALU.mult))
            chain(nc.vector, s_dve,
                  nc.vector.tensor_scalar_mul(stat[0:1, 0:NH], stat[0:1, 0:NH],
                                              1.0 / 128.0))
            nc.scalar.wait_ge(s_dve.h, s_dve.n)
            chain(nc.scalar, s_act,
                  nc.scalar.activation(recb[0][0:1, 0:NH], stat[0:1, 0:NH],
                                       AF.Ln))
            chain(nc.scalar, s_act,
                  nc.scalar.activation(stat[0:1, 0:NH], recb[0][0:1, 0:NH],
                                       AF.Exp, scale=0.5))
            nc.vector.wait_ge(s_act.h, s_act.n)
            chain(nc.vector, s_dve,
                  nc.vector.tensor_scalar_mul(stat[0:1, 0:NH], stat[0:1, 0:NH],
                                              -1.0))
            nc.tensor.wait_ge(s_dve.h, s_dve.n)
            mm = nc.tensor.matmul(pA[:, 0:NH], ones_f[0:1, :],
                                  stat[0:1, 0:NH], start=True, stop=True)
            s_pe.inc(mm)
            nc.vector.wait_ge(s_pe.h, s_pe.n)
            chain(nc.vector, s_dve,
                  nc.vector.tensor_copy(negB[:, :], pA[:, 0:NH]))

            kv_done = {}
            wo_gate = {}
            pend_wo = []
            zt_done = {}
            exp_done = {}
            msk_done = {}
            avdn_done = {}
            head_pe_done = {}
            q_loaded = {}

            def issue_gathers(h):
                buf = h % 2
                nc.gpsimd.wait_ge(s_cc.h, 1)
                if h >= 2:
                    nc.gpsimd.wait_ge(s_pe.h, head_pe_done[h - 2])
                for s2 in range(2):
                    i2 = nc.gpsimd.indirect_dma_start(
                        kfull[buf][:, s2 * T:(s2 + 1) * T], None,
                        kvK_out[:, :],
                        bass.IndirectOffsetOnAxis(
                            idx_s[:, 2 * h + s2:2 * h + s2 + 1], 0))
                    s_kv.inc(i2, 16)
                nc.gpsimd.wait_ge(s_cc.h, 3)
                for s2 in range(2):
                    i2 = nc.gpsimd.indirect_dma_start(
                        vE[buf][:, s2 * T:(s2 + 1) * T], None,
                        kvV_out[:, :],
                        bass.IndirectOffsetOnAxis(
                            idx_s[:, 2 * h + s2:2 * h + s2 + 1], 0))
                    s_kv.inc(i2, 16)
                kv_done[h] = s_kv.n

            def issue_qload(h):
                buf = h % 2
                if h >= 2:
                    nc.sync.wait_ge(s_pe.h, head_pe_done[h - 2])
                i2 = nc.sync.dma_start(qsb[buf][:, :],
                                       qstage_d[h * P:(h + 1) * P, :])
                q_loaded[h] = s_hd.inc(i2, 16)

            issue_gathers(0)
            issue_gathers(1)
            issue_qload(0)
            issue_qload(1)

            for h in range(NH):
                buf = h % 2
                nc.tensor.wait_ge(s_kv.h, kv_done[h])
                if h >= 2:
                    nc.tensor.wait_ge(s_pe.h, head_pe_done[h - 2])
                act_vt0 = s_act.n
                for kt in range(16):
                    pt = pT[kt % 2]
                    if kt >= 2:
                        nc.tensor.wait_ge(s_act.h, act_vt0 + kt - 1)
                    s_pe.inc(nc.tensor.transpose(
                        pt[:, 0:P], vE[buf][:, kt * P:(kt + 1) * P],
                        ident_b[:, :]))
                    nc.scalar.wait_ge(s_pe.h, s_pe.n)
                    s_act.inc(nc.scalar.activation(
                        vt[buf][:, kt * P:(kt + 1) * P], pt[:, 0:P], AF.Copy))

                if pend_wo:
                    g2p, ztg, fn = pend_wo.pop(0)
                    fn(g2p, ztg)

                nc.tensor.wait_ge(s_hd.h, q_loaded[h])
                def emit_sc(it):
                    kt, half = divmod(it, 2)
                    psc = pA if it % 2 == 0 else pB
                    if it >= 2:
                        nc.tensor.wait_ge(s_act.h, exp_done[(h, it - 2)])
                    else:
                        nc.tensor.wait_ge(s_dve.h, s_dve.n)
                    mm = nc.tensor.matmul(
                        psc[:, :], kfull[buf][:, kt * P:(kt + 1) * P],
                        qsb[buf][:, half * HB:(half + 1) * HB],
                        start=True, stop=False)
                    g0 = 2048 - kt * P + half * HB
                    mm = nc.tensor.matmul(psc[:, :], ident_b[:, :],
                                          bigm[:, g0:g0 + HB],
                                          start=False, stop=True)
                    pe_n = s_pe.inc(mm)
                    nc.scalar.wait_ge(s_pe.h, pe_n)
                    if it >= 3:
                        nc.scalar.wait_ge(s_pe.h, avdn_done[(h, it - 3)])
                    a2 = nc.scalar.activation(expb[it % 3][:, :], psc[:, :],
                                              AF.Exp, scale=SCALE_QK,
                                              bias=negB[:, h:h + 1])
                    exp_done[(h, it)] = s_act.inc(a2)

                def emit_av(it):
                    kt, half = divmod(it, 2)
                    nc.tensor.wait_ge(s_act.h, exp_done[(h, it)])
                    mm = nc.tensor.matmul(
                        pZ[half][:, :], vt[buf][:, kt * P:(kt + 1) * P],
                        expb[it % 3][:, :],
                        start=(kt == 0), stop=(kt == 15))
                    mm2 = nc.tensor.matmul(
                        pD[half][:, :], ones_b[:, :], expb[it % 3][:, :],
                        start=(kt == 0), stop=(kt == 15))
                    avdn_done[(h, it)] = s_pe.inc(mm2)

                for it in range(32):
                    emit_sc(it)
                    if it >= 2:
                        emit_av(it - 2)
                emit_av(30)
                emit_av(31)
                head_pe_done[h] = s_pe.n
                if h + 2 < NH:
                    issue_gathers(h + 2)
                    issue_qload(h + 2)
                nc.vector.wait_ge(s_pe.h, head_pe_done[h])
                for half in range(2):
                    chain(nc.vector, s_dve,
                          nc.vector.reciprocal(recb[half][:, :],
                                               pD[half][:, :]))
                    s_dve.inc(nc.vector.tensor_tensor(
                        ztb[h % 2][:, half * HB:(half + 1) * HB],
                        pZ[half][:, :], recb[half][:, :], ALU.mult))
                zt_done[h] = s_dve.n

                if h % 2 == 0:
                    g2n = h // 2
                    for hh in range(2):
                        if g2n >= 1:
                            nc.gpsimd.wait_ge(s_pe.h, wo_gate[g2n - 1])
                        i2 = nc.gpsimd.dma_start(wob[hh][:, :],
                                                 wo_d[g2n * 2 + hh])
                        sw[4 + hh].inc(i2, 16)
                def emit_wo(g2, zt_gate):
                    nc.tensor.wait_ge(s_dve.h, zt_gate)
                    nc.tensor.wait_ge(sw[4].h, sw[4].n)
                    nc.tensor.wait_ge(sw[5].h, sw[5].n)
                    rot = [pA, pB, pZ[0], pZ[1]]
                    ev_ms = {}
                    for dc in range(NDC):
                        for half in range(2):
                            pr2 = rot[(dc * 2 + half) % 4]
                            if dc * 2 + half >= 4:
                                nc.tensor.wait_ge(s_dve.h,
                                                  ev_ms[dc * 2 + half - 4])
                            for hh in range(2):
                                mm = nc.tensor.matmul(
                                    pr2[:, :],
                                    wob[hh][:, dc * P:(dc + 1) * P],
                                    ztb[hh][:, half * HB:(half + 1) * HB],
                                    start=(hh == 0), stop=(hh == 1))
                            s_pe.inc(mm)
                            nc.vector.wait_ge(s_pe.h, s_pe.n)
                            if g2 == 0:
                                ev = nc.vector.scalar_tensor_tensor(
                                    xr[:, dc * T + half * HB:
                                       dc * T + (half + 1) * HB],
                                    pr2[:, :], bo_s[:, dc:dc + 1],
                                    xr[:, dc * T + half * HB:
                                       dc * T + (half + 1) * HB],
                                    ALU.add, ALU.add)
                            else:
                                ev = nc.vector.tensor_tensor(
                                    xr[:, dc * T + half * HB:
                                       dc * T + (half + 1) * HB],
                                    xr[:, dc * T + half * HB:
                                       dc * T + (half + 1) * HB],
                                    pr2[:, :], ALU.add)
                            ev_ms[dc * 2 + half] = s_dve.inc(ev)
                    wo_gate[g2] = s_pe.n

                if h % 2 == 1:
                    pend_wo.append((h // 2, zt_done[h], emit_wo))
            g2l, ztg, fn = pend_wo.pop(0)
            fn(g2l, ztg)
            nc.vector.wait_ge(s_dve.h, s_dve.n)
        barrier()

    layer_norm()
    ffn_call()
    if n_sub >= 2:
        layer_norm()
        attn()
    if n_sub >= 3:
        layer_norm()
        ffn_call()

    for s in sems.values():
        if s.n:
            nc.sync.wait_ge(s.h, s.n)
    out = nc.sync.dma_start(
        y_d.rearrange("(dc p) t -> p dc t", p=P),
        xr[:, :].rearrange("p (dc t) -> p dc t", t=T))
    s_out.inc(out, 16)
    nc.sync.wait_ge(s_out.h, s_out.n)
    st.close()
    return nc


def _stage_inputs(inputs):
    f32 = np.float32
    asf = lambda a: np.ascontiguousarray(np.asarray(a, dtype=f32))
    x = asf(inputs["L0_attn_x"])
    w1T = asf(inputs["L0_ffn_w1"]).transpose(0, 2, 1)
    sguT = asf(inputs["L0_ffn_shared_gate_up_w"]).T

    def pack_w1(col0):
        a = w1T[:, :, col0:col0 + IEX].reshape(8, NDC, P, NIC, P)
        a = a.transpose(0, 3, 2, 1, 4).reshape(8 * NIC * P, NDC * P)
        b = sguT[:, col0 // IEX * ISH:col0 // IEX * ISH + ISH]
        b = b.reshape(NDC, P, 4, NIC, P).transpose(2, 3, 1, 0, 4)
        return np.ascontiguousarray(
            np.concatenate([a, b.reshape(4 * NIC * P, NDC * P)], axis=0))

    w2T = asf(inputs["L0_ffn_w2"]).transpose(0, 2, 1)
    a2 = w2T.reshape(8, NIC, P, 8, 256).transpose(0, 3, 2, 1, 4)
    sdT = asf(inputs["L0_ffn_shared_down_w"]).T
    b2 = sdT.reshape(4, NIC, P, 8, 256).transpose(0, 3, 2, 1, 4)
    w2p = np.ascontiguousarray(np.concatenate(
        [a2.reshape(8 * 8 * P, NIC * 256), b2.reshape(4 * 8 * P, NIC * 256)],
        axis=0))

    def pack_qkv(w):
        a = asf(w).reshape(NH, NDC, P, EH).transpose(0, 2, 1, 3)
        return np.ascontiguousarray(a.reshape(NH * P, NDC * EH))

    shared = {
        "w1gp": pack_w1(0),
        "w1up": pack_w1(IEX),
        "w2p": w2p,
        "gw9T": np.ascontiguousarray(
            np.concatenate([asf(inputs["L0_ffn_gate_w"]),
                            asf(inputs["L0_ffn_shared_gate_w"]),
                            np.zeros((7, D), f32)], axis=0).T),
        "wqp": pack_qkv(inputs["L0_attn_W_Q"]),
        "wkp": pack_qkv(inputs["L0_attn_W_K"]),
        "wvp": pack_qkv(inputs["L0_attn_W_V"]),
        "wo": asf(inputs["L0_attn_W_O"]),
        "bqT": np.ascontiguousarray(asf(inputs["L0_attn_b_Q"]).T),
        "bkT": np.ascontiguousarray(asf(inputs["L0_attn_b_K"]).T),
        "bvT": np.ascontiguousarray(asf(inputs["L0_attn_b_V"]).T),
        "bo": np.ascontiguousarray(asf(inputs["L0_attn_b_O"]).reshape(NDC, P).T),
    }
    per_core = []
    for r in range(8):
        b, hh = r // 2, r % 2
        shard = np.ascontiguousarray(x[b, hh * T:(hh + 1) * T, :].T)
        idx = np.empty((P, 2 * NH), np.int32)
        for h in range(NH):
            idx[:, 2 * h + 0] = (2 * b) * D + h * P + np.arange(P)
            idx[:, 2 * h + 1] = (2 * b + 1) * D + h * P + np.arange(P)
        m = {"xT": shard, "qbase": np.array([[hh * T]], f32), "kvidx": idx}
        m.update(shared)
        per_core.append(m)
    return per_core


def kernel(**inputs):
    per_core = _stage_inputs(inputs)
    nc = build_program(n_sub=3)
    res = run_bass_kernel_spmd(nc, per_core, core_ids=list(range(8)))
    out = np.empty((4, 2048, D), np.float32)
    for r in range(8):
        b, hh = r // 2, r % 2
        out[b, hh * T:(hh + 1) * T, :] = np.asarray(res.results[r]["y"]).T
    return out


# version 06
# speedup vs baseline: 1.0188x; 1.0080x over previous
import contextlib
import numpy as np
import concourse.bass as bass
import concourse.mybir as mybir
from concourse.bass_utils import run_bass_kernel_spmd

F32 = mybir.dt.float32
BF16 = mybir.dt.bfloat16
I32 = mybir.dt.int32
AF = mybir.ActivationFunctionType
ALU = mybir.AluOpType
AX = mybir.AxisListType.X

P = 128
D = 2048
NDC = 16
T = 1024
NH = 16
EH = 128
IEX = 1408
NIC = 11
NPE = 12
ISH = 5632
SCALE_QK = float(1.0 / np.sqrt(128.0))
NEG_BIG = -100000.0
LN_EPS = 1e-5
HB = 512


class Sem:
    def __init__(self, st, nc, name):
        self.h = st.enter_context(nc.semaphore(name))
        self.n = 0

    def inc(self, instr, k=1):
        instr.then_inc(self.h, k)
        self.n += k
        return self.n


def build_program(n_sub=3, debug=False):
    nc = bass.Bass("TRN2", target_bir_lowering=False)
    uid_ctr = [0]
    def uid():
        uid_ctr[0] += 1
        return "_u%d" % uid_ctr[0]

    xT_d = nc.dram_tensor("xT", [D, T], F32, kind="ExternalInput").ap()
    qb_d = nc.dram_tensor("qbase", [1, 1], F32, kind="ExternalInput").ap()
    idx_d = nc.dram_tensor("kvidx", [P, 2 * NH], I32, kind="ExternalInput").ap()
    w1gp_d = nc.dram_tensor("w1gp", [NPE * NIC * P, NDC * P], F32,
                            kind="ExternalInput").ap()
    w1up_d = nc.dram_tensor("w1up", [NPE * NIC * P, NDC * P], F32,
                            kind="ExternalInput").ap()
    w2p_d = nc.dram_tensor("w2p", [NPE * 8 * P, NIC * 256], F32,
                           kind="ExternalInput").ap()
    gw9T_d = nc.dram_tensor("gw9T", [D, 16], F32, kind="ExternalInput").ap()
    wq_d = nc.dram_tensor("wqp", [NH * P, NDC * EH], F32, kind="ExternalInput").ap()
    wk_d = nc.dram_tensor("wkp", [NH * P, NDC * EH], F32, kind="ExternalInput").ap()
    wv_d = nc.dram_tensor("wvp", [NH * P, NDC * EH], F32, kind="ExternalInput").ap()
    wo_d = nc.dram_tensor("wo", [NH, EH, D], F32, kind="ExternalInput").ap()
    bqT_d = nc.dram_tensor("bqT", [P, NH], F32, kind="ExternalInput").ap()
    bkT_d = nc.dram_tensor("bkT", [P, NH], F32, kind="ExternalInput").ap()
    bvT_d = nc.dram_tensor("bvT", [P, NH], F32, kind="ExternalInput").ap()
    bo_d = nc.dram_tensor("bo", [P, NDC], F32, kind="ExternalInput").ap()
    y_d = nc.dram_tensor("y", [D, T], F32, kind="ExternalOutput").ap()
    dbg_d = (nc.dram_tensor("dbg", [P, 4096], F32, kind="ExternalOutput").ap()
             if debug else None)
    dbg2_d = (nc.dram_tensor("dbg2", [NH * P, T], F32, kind="ExternalOutput").ap()
              if debug else None)

    kvK_in = nc.dram_tensor("kvK_in", [D, T], BF16).ap()
    kvV_in = nc.dram_tensor("kvV_in", [D, T], BF16).ap()
    kvK_out = nc.dram_tensor("kvK_out", [2 * D, T], BF16).ap()
    kvV_out = nc.dram_tensor("kvV_out", [2 * D, T], BF16).ap()
    qstage_d = nc.dram_tensor("qstage", [D, T], BF16).ap()
    kstat_in = nc.dram_tensor("kstat_in", [1, NH], F32).ap()
    kstat_out = nc.dram_tensor("kstat_out", [1, NH], F32).ap()

    st = contextlib.ExitStack()
    sb = lambda name, shape, dt: st.enter_context(nc.sbuf_tensor(name, shape, dt))

    xr = sb("xr", [P, NDC * T], F32)
    xl = sb("xl", [P, NDC * T], BF16)
    ones_b = sb("ones_b", [P, P], BF16)
    ones_f = sb("ones_f", [1, P], F32)
    half_f = sb("half_f", [1, P], F32)
    half_b = sb("half_b", [1, P], BF16)
    ident_f = sb("ident_f", [P, P], F32)
    ident_b = sb("ident_b", [P, P], BF16)
    iota_p = sb("iota_p", [P, 1], F32)
    qbB = sb("qbB", [P, 1], F32)
    gw9 = sb("gw9", [P, NDC * 16], BF16)
    bq_s = sb("bq_s", [P, NH], F32)
    bk_s = sb("bk_s", [P, NH], F32)
    bv_s = sb("bv_s", [P, NH], F32)
    bo_s = sb("bo_s", [P, NDC], F32)
    qb_s = sb("qb_s", [1, 1], F32)
    idx_s = sb("idx_s", [P, 2 * NH], I32)
    iota_sml = sb("iota_sml", [P, P], F32)
    bigm = sb("bigm", [P, 3072], BF16)

    sems = {}
    for name in ["pe", "dve", "act", "pool", "hd", "cc", "out", "kv",
                 "w0", "w1", "w2", "w3", "w4", "w5"]:
        sems[name] = Sem(st, nc, "s_" + name)
    s_pe, s_dve, s_act = sems["pe"], sems["dve"], sems["act"]
    s_pool, s_hd, s_cc = sems["pool"], sems["hd"], sems["cc"]
    s_out, s_kv = sems["out"], sems["kv"]
    sw = [sems["w%d" % i] for i in range(6)]

    ENGS = [nc.tensor, nc.vector, nc.scalar, nc.gpsimd, nc.sync]

    def barrier():
        for eng in ENGS:
            for s in sems.values():
                if s.n:
                    eng.wait_ge(s.h, s.n)

    def chain(eng, sem, instr, k=1):
        sem.inc(instr, k)
        eng.wait_ge(sem.h, sem.n)

    i = nc.sync.dma_start(
        xr[:, :].rearrange("p (dc t) -> p dc t", t=T),
        xT_d.rearrange("(dc p) t -> p dc t", p=P))
    s_hd.inc(i, 16)
    for dst, src in [(qb_s[:, :], qb_d), (idx_s[:, :], idx_d),
                     (bq_s[:, :], bqT_d), (bk_s[:, :], bkT_d),
                     (bv_s[:, :], bvT_d)]:
        s_hd.inc(nc.sync.dma_start(dst, src), 16)
    s_pool.inc(nc.gpsimd.dma_start(bo_s[:, :], bo_d), 16)
    s_pool.inc(nc.gpsimd.dma_start(
        gw9[:, :].rearrange("p (dc w) -> p dc w", w=16),
        gw9T_d.rearrange("(dc p) w -> p dc w", p=P)), 16)
    chain(nc.gpsimd, s_pool,
          nc.gpsimd.iota(iota_p[:, :], pattern=[[0, 1]], base=0,
                         channel_multiplier=1,
                         allow_small_or_imprecise_dtypes=True))
    chain(nc.gpsimd, s_pool,
          nc.gpsimd.iota(iota_sml[:, :], pattern=[[1, P]], base=0,
                         channel_multiplier=0,
                         allow_small_or_imprecise_dtypes=True))
    nc.vector.memset(ones_b[:, :], 1.0)
    nc.vector.memset(ones_f[:, :], 1.0)
    nc.vector.memset(half_f[:, :], 0.5)
    nc.vector.memset(half_b[:, :], 0.5)
    nc.vector.wait_ge(s_pool.h, s_pool.n)
    chain(nc.vector, s_dve,
          nc.vector.tensor_scalar(ident_f[:, :], iota_sml[:, :], iota_p[:, 0:1],
                                  None, ALU.is_equal))
    chain(nc.vector, s_dve, nc.vector.tensor_copy(ident_b[:, :], ident_f[:, :]))
    with nc.psum_tensor("pqb2", [P, 512], F32) as pqb:
        nc.tensor.wait_ge(s_hd.h, s_hd.n)
        nc.tensor.wait_ge(s_dve.h, s_dve.n)
        mm = nc.tensor.matmul(pqb[:, 0:1], ones_f[0:1, :], qb_s[0:1, 0:1],
                              start=True, stop=True)
        s_pe.inc(mm)
        nc.vector.wait_ge(s_pe.h, s_pe.n)
        chain(nc.vector, s_dve, nc.vector.tensor_copy(qbB[:, :], pqb[:, 0:1]))
    with nc.sbuf_tensor("iota3k_tmp", [P, 3072], F32) as iota3k, \
         nc.sbuf_tensor("tcol_tmp", [P, 1], F32) as tcol:
        chain(nc.gpsimd, s_pool,
              nc.gpsimd.iota(iota3k[:, :], pattern=[[1, 3072]], base=-2048,
                             channel_multiplier=0,
                             allow_small_or_imprecise_dtypes=True))
        nc.vector.wait_ge(s_pool.h, s_pool.n)
        nc.vector.wait_ge(s_dve.h, s_dve.n)
        nc.vector.wait_ge(s_pe.h, s_pe.n)
        chain(nc.vector, s_dve,
              nc.vector.tensor_tensor(tcol[:, :], iota_p[:, :], qbB[:, :],
                                      ALU.subtract))
        chain(nc.vector, s_dve,
              nc.vector.tensor_scalar(bigm[:, :], iota3k[:, :], tcol[:, 0:1],
                                      NEG_BIG, ALU.is_lt, ALU.mult))
    barrier()

    def layer_norm():
        with contextlib.ExitStack() as ph:
            _u = uid()
            psb = lambda n_, s_, d_: ph.enter_context(nc.psum_tensor(n_ + _u, s_, d_))
            lsb = lambda n_, s_, d_: ph.enter_context(nc.sbuf_tensor(n_ + _u, s_, d_))
            p_s = [psb("pls%d" % i2, [P, HB], F32) for i2 in range(2)]
            p_q = [psb("plq%d" % i2, [P, HB], F32) for i2 in range(2)]
            xb_t = [lsb("lnxb%d" % i2, [P, T], BF16) for i2 in range(2)]
            sq_t = [lsb("lnsq%d" % i2, [P, T], BF16) for i2 in range(2)]
            mB = lsb("lnmB", [P, T], F32)
            v1 = lsb("lnv1", [P, T], F32)
            rstd = lsb("lnrstd", [P, T], F32)
            tt32 = [lsb("lnt%d" % i2, [P, T], F32) for i2 in range(4)]

            dve0, act0, pe0 = s_dve.n, s_act.n, s_pe.n
            for dc in range(NDC):
                b = dc % 2
                if dc >= 2:
                    nc.vector.wait_ge(s_pe.h, pe0 + 4 * (dc - 1))
                    nc.scalar.wait_ge(s_pe.h, pe0 + 4 * (dc - 1))
                s_dve.inc(nc.vector.tensor_copy(xb_t[b][:, :],
                                                xr[:, dc * T:(dc + 1) * T]))
                s_act.inc(nc.scalar.activation(sq_t[b][:, :],
                                               xr[:, dc * T:(dc + 1) * T],
                                               AF.Square))
                nc.tensor.wait_ge(s_dve.h, dve0 + dc + 1)
                nc.tensor.wait_ge(s_act.h, act0 + dc + 1)
                for half in range(2):
                    s_pe.inc(nc.tensor.matmul(
                        p_s[half][:, :], ones_b[:, :],
                        xb_t[b][:, half * HB:(half + 1) * HB],
                        start=(dc == 0), stop=(dc == NDC - 1)))
                    s_pe.inc(nc.tensor.matmul(
                        p_q[half][:, :], ones_b[:, :],
                        sq_t[b][:, half * HB:(half + 1) * HB],
                        start=(dc == 0), stop=(dc == NDC - 1)))
            nc.scalar.wait_ge(s_pe.h, s_pe.n)
            for half in range(2):
                sl = slice(half * HB, (half + 1) * HB)
                s_act.inc(nc.scalar.activation(mB[:, sl], p_s[half][:, :],
                                               AF.Copy, scale=1.0 / D))
                s_act.inc(nc.scalar.activation(v1[:, sl], p_q[half][:, :],
                                               AF.Copy, scale=1.0 / D))
            nc.vector.wait_ge(s_act.h, s_act.n)
            chain(nc.vector, s_dve,
                  nc.vector.tensor_tensor(tt32[0][:, :], mB[:, :], mB[:, :],
                                          ALU.mult))
            chain(nc.vector, s_dve,
                  nc.vector.tensor_tensor(v1[:, :], v1[:, :], tt32[0][:, :],
                                          ALU.subtract))
            chain(nc.vector, s_dve,
                  nc.vector.tensor_scalar_add(v1[:, :], v1[:, :], LN_EPS))
            nc.scalar.wait_ge(s_dve.h, s_dve.n)
            chain(nc.scalar, s_act,
                  nc.scalar.activation(tt32[1][:, :], v1[:, :], AF.Ln))
            chain(nc.scalar, s_act,
                  nc.scalar.activation(rstd[:, :], tt32[1][:, :], AF.Exp,
                                       scale=-0.5))
            nc.vector.wait_ge(s_act.h, s_act.n)
            nc.gpsimd.wait_ge(s_act.h, s_act.n)
            dcnt = {}
            pcnt = {}
            for dc in range(NDC):
                b = dc % 2
                if dc < 10:
                    if dc >= 2:
                        nc.vector.wait_ge(s_dve.h, dcnt[dc - 2])
                    s_dve.inc(nc.vector.tensor_tensor(
                        tt32[b][:, :], xr[:, dc * T:(dc + 1) * T],
                        mB[:, :], ALU.subtract))
                    nc.vector.wait_ge(s_dve.h, s_dve.n)
                    s_dve.inc(nc.vector.tensor_tensor(
                        xl[:, dc * T:(dc + 1) * T], tt32[b][:, :],
                        rstd[:, :], ALU.mult))
                    dcnt[dc] = s_dve.n
                else:
                    if dc >= 12:
                        nc.gpsimd.wait_ge(s_pool.h, pcnt[dc - 2])
                    s_pool.inc(nc.gpsimd.tensor_tensor(
                        tt32[2 + b][:, :], xr[:, dc * T:(dc + 1) * T],
                        mB[:, :], ALU.subtract))
                    nc.gpsimd.wait_ge(s_pool.h, s_pool.n)
                    s_pool.inc(nc.gpsimd.tensor_tensor(
                        xl[:, dc * T:(dc + 1) * T], tt32[2 + b][:, :],
                        rstd[:, :], ALU.mult))
                    pcnt[dc] = s_pool.n
            nc.vector.wait_ge(s_dve.h, s_dve.n)
            nc.vector.wait_ge(s_pool.h, s_pool.n)
        barrier()

    def w1_srcs(e, ic):
        r0 = (e * NIC + ic) * P
        return w1gp_d[r0:r0 + P, :], w1up_d[r0:r0 + P, :]

    def w2_src(e, dcg):
        r0 = (e * 8 + dcg) * P
        return w2p_d[r0:r0 + P, :]

    def ffn_call():
        with contextlib.ExitStack() as ph:
            _u = uid()
            psb = lambda n_, s_, d_: ph.enter_context(nc.psum_tensor(n_ + _u, s_, d_))
            lsb = lambda n_, s_, d_: ph.enter_context(nc.sbuf_tensor(n_ + _u, s_, d_))
            pb = [psb("pf%d" % i2, [P, HB], F32) for i2 in range(8)]
            w1g = [lsb("w1g%d" % i2, [P, NDC * P], BF16) for i2 in range(2)]
            w1u = [lsb("w1u%d" % i2, [P, NDC * P], BF16) for i2 in range(2)]
            w2b = [lsb("w2b%d" % i2, [P, NIC * 256], BF16) for i2 in range(4)]
            heb = lsb("heb", [P, 2 * NIC * T], BF16)
            cBb = [lsb("cB%d" % i2, [P, T], F32) for i2 in range(2)]
            sgb = [lsb("sg%d" % i2, [P, HB], BF16) for i2 in range(2)]
            tmpb = [lsb("tmp%d" % i2, [P, HB], BF16) for i2 in range(2)]
            lg9 = lsb("lg9", [P, 8 * 9], F32)
            cm9 = lsb("cm9", [P, 8 * 9], F32)
            crow = lsb("crow", [1, T], BF16)
            rsc = lsb("rsc", [P, 8 * 24], F32)

            act_r0 = s_act.n
            for tt in range(8):
                pr = pb[tt % 2]
                if tt >= 2:
                    nc.tensor.wait_ge(s_act.h, act_r0 + (tt - 1))
                for dc in range(NDC):
                    mm = nc.tensor.matmul(
                        pr[:, 0:9],
                        xl[:, dc * T + tt * P:dc * T + (tt + 1) * P],
                        gw9[:, dc * 16:dc * 16 + 9],
                        start=(dc == 0), stop=(dc == NDC - 1))
                s_pe.inc(mm)
                nc.scalar.wait_ge(s_pe.h, s_pe.n)
                s_act.inc(nc.scalar.activation(lg9[:, tt * 9:(tt + 1) * 9],
                                               pr[:, 0:9], AF.Copy))
            nc.vector.wait_ge(s_act.h, s_act.n)
            for tt in range(8):
                lg = lg9[:, tt * 9:tt * 9 + 8]
                sc = rsc[:, tt * 24:(tt + 1) * 24]
                chain(nc.vector, s_dve,
                      nc.vector.reduce_max(sc[:, 0:1], lg, axis=AX))
                chain(nc.vector, s_dve,
                      nc.vector.tensor_scalar_mul(sc[:, 1:2], sc[:, 0:1], -1.0))
                nc.scalar.wait_ge(s_dve.h, s_dve.n)
                chain(nc.scalar, s_act,
                      nc.scalar.activation(sc[:, 2:10], lg, AF.Exp,
                                           bias=sc[:, 1:2]))
                nc.vector.wait_ge(s_act.h, s_act.n)
                chain(nc.vector, s_dve,
                      nc.vector.reduce_sum(sc[:, 10:11], sc[:, 2:10], axis=AX))
                chain(nc.vector, s_dve,
                      nc.vector.reciprocal(sc[:, 11:12], sc[:, 10:11]))
                chain(nc.vector, s_dve,
                      nc.vector.tensor_scalar(sc[:, 2:10], sc[:, 2:10],
                                              sc[:, 11:12], None, ALU.mult))
                chain(nc.vector, s_dve,
                      nc.vector.reduce_max(sc[:, 12:13], sc[:, 2:10], axis=AX))
                chain(nc.vector, s_dve,
                      nc.vector.tensor_scalar(sc[:, 16:24], sc[:, 2:10],
                                              sc[:, 12:13], None, ALU.is_lt))
                chain(nc.vector, s_dve,
                      nc.vector.tensor_tensor(sc[:, 16:24], sc[:, 2:10],
                                              sc[:, 16:24], ALU.mult))
                chain(nc.vector, s_dve,
                      nc.vector.reduce_max(sc[:, 13:14], sc[:, 16:24], axis=AX))
                chain(nc.vector, s_dve,
                      nc.vector.tensor_scalar(sc[:, 16:24], sc[:, 2:10],
                                              sc[:, 13:14], None, ALU.is_ge))
                chain(nc.vector, s_dve,
                      nc.vector.tensor_tensor(sc[:, 14:15], sc[:, 12:13],
                                              sc[:, 13:14], ALU.add))
                chain(nc.vector, s_dve,
                      nc.vector.reciprocal(sc[:, 15:16], sc[:, 14:15]))
                chain(nc.vector, s_dve,
                      nc.vector.tensor_tensor(sc[:, 16:24], sc[:, 2:10],
                                              sc[:, 16:24], ALU.mult))
                chain(nc.vector, s_dve,
                      nc.vector.tensor_scalar(cm9[:, tt * 9:tt * 9 + 8],
                                              sc[:, 16:24], sc[:, 15:16], None,
                                              ALU.mult))
                nc.scalar.wait_ge(s_dve.h, s_dve.n)
                chain(nc.scalar, s_act,
                      nc.scalar.activation(cm9[:, tt * 9 + 8:tt * 9 + 9],
                                           lg9[:, tt * 9 + 8:tt * 9 + 9],
                                           AF.Sigmoid))
            nc.vector.wait_ge(s_act.h, s_act.n)

            def build_cB(e, slot, gate_dve):
                ce = e if e < 8 else 8
                nc.tensor.wait_ge(s_dve.h, gate_dve)
                nc.tensor.wait_ge(s_act.h, s_act.n)
                d_base = s_dve.n
                for tt in range(8):
                    pt = pb[6 + tt % 2]
                    if tt >= 2:
                        nc.tensor.wait_ge(s_dve.h, d_base + tt - 1)
                    tr = nc.tensor.transpose(
                        pt[0:1, 0:P], cm9[:, tt * 9 + ce:tt * 9 + ce + 1],
                        ident_f[:, :])
                    s_pe.inc(tr)
                    nc.vector.wait_ge(s_pe.h, s_pe.n)
                    s_dve.inc(nc.vector.tensor_copy(
                        crow[0:1, tt * P:(tt + 1) * P], pt[0:1, 0:P]))
                nc.tensor.wait_ge(s_dve.h, s_dve.n)
                for half in range(2):
                    mm = nc.tensor.matmul(pb[6 + half][:, :], half_b[0:1, :],
                                          crow[0:1, half * HB:(half + 1) * HB],
                                          start=True, stop=True)
                    s_pe.inc(mm)
                nc.vector.wait_ge(s_pe.h, s_pe.n)
                nc.vector.wait_ge(s_pool.h, s_pool.n)
                for half in range(2):
                    s_dve.inc(nc.vector.tensor_copy(
                        cBb[slot][:, half * HB:(half + 1) * HB],
                        pb[6 + half][:, :]))
                return s_dve.n

            w1_pe_done = {}
            silu_done = {}
            tmp_done = {}
            he_done = {}
            w2_pe_done = {}
            w2_ev_done = {}
            wo_gate = {}

            def issue_w1(e, ic, gate_pe):
                buf = (e * NIC + ic) % 2
                g_src, u_src = w1_srcs(e, ic)
                if gate_pe is not None:
                    nc.gpsimd.wait_ge(s_pe.h, gate_pe)
                i1 = nc.gpsimd.dma_start(w1g[buf][:, :], g_src)
                sw[buf].inc(i1, 16)
                i2 = nc.gpsimd.dma_start(w1u[buf][:, :], u_src)
                sw[buf].inc(i2, 16)

            def issue_w2(g, dcg, gate_pe):
                for s01 in range(2):
                    buf = (dcg % 2) * 2 + s01
                    if gate_pe is not None:
                        nc.gpsimd.wait_ge(s_pe.h, gate_pe)
                    i1 = nc.gpsimd.dma_start(w2b[buf][:, :],
                                             w2_src(2 * g + s01, dcg))
                    sw[2 + buf].inc(i1, 16)

            issue_w1(0, 0, None)
            issue_w1(0, 1, None)
            build_cB(0, 0, s_dve.n)
            build_cB(1, 1, s_dve.n)
            issue_w2(0, 0, None)
            issue_w2(0, 1, None)

            step = 0
            for g in range(6):
                for s01 in range(2):
                    e = 2 * g + s01
                    for ic in range(NIC):
                        gbuf = (e * NIC + ic) % 2
                        nc.tensor.wait_ge(sw[gbuf].h, sw[gbuf].n)
                        for half in range(2):
                            gps = pb[step % 2]
                            ups = pb[2 + step % 2]
                            if step >= 2:
                                nc.tensor.wait_ge(s_act.h, silu_done[step - 2])
                                nc.tensor.wait_ge(s_dve.h, tmp_done[step - 2])
                            for dc in range(NDC):
                                mm = nc.tensor.matmul(
                                    gps[:, :],
                                    w1g[gbuf][:, dc * P:(dc + 1) * P],
                                    xl[:, dc * T + half * HB:
                                       dc * T + (half + 1) * HB],
                                    start=(dc == 0), stop=(dc == NDC - 1))
                            for dc in range(NDC):
                                mm = nc.tensor.matmul(
                                    ups[:, :],
                                    w1u[gbuf][:, dc * P:(dc + 1) * P],
                                    xl[:, dc * T + half * HB:
                                       dc * T + (half + 1) * HB],
                                    start=(dc == 0), stop=(dc == NDC - 1))
                            s_pe.inc(mm)
                            w1_pe_done[(e, ic)] = s_pe.n
                            nc.scalar.wait_ge(s_pe.h, s_pe.n)
                            if step >= 2:
                                nc.scalar.wait_ge(s_dve.h, tmp_done[step - 2])
                            a = nc.scalar.activation(sgb[step % 2][:, :],
                                                     gps[:, :], AF.Silu)
                            silu_done[step] = s_act.inc(a)
                            nc.vector.wait_ge(s_act.h, silu_done[step])
                            if step >= 2:
                                nc.vector.wait_ge(s_pool.h, he_done[step - 2])
                            v = nc.vector.scalar_tensor_tensor(
                                tmpb[step % 2][:, :], sgb[step % 2][:, :], 1.0,
                                ups[:, :], ALU.mult, ALU.mult)
                            tmp_done[step] = s_dve.inc(v)
                            nc.gpsimd.wait_ge(s_dve.h, tmp_done[step])
                            hs = (s01 * NIC + ic) * T + half * HB
                            pl = nc.gpsimd.tensor_tensor(
                                heb[:, hs:hs + HB], tmpb[step % 2][:, :],
                                cBb[s01][:, half * HB:(half + 1) * HB],
                                ALU.mult)
                            he_done[step] = s_pool.inc(pl)
                            step += 1
                        nx = e * NIC + ic + 2
                        if nx < NPE * NIC:
                            ne, nic2 = divmod(nx, NIC)
                            issue_w1(ne, nic2, w1_pe_done[(e, ic)])
                last_he = he_done[step - 1]
                for dcg in range(8):
                    setb = (dcg % 2) * 4
                    nc.tensor.wait_ge(s_pool.h, last_he)
                    for s01 in range(2):
                        wb = (dcg % 2) * 2 + s01
                        nc.tensor.wait_ge(sw[2 + wb].h, sw[2 + wb].n)
                    if dcg >= 2:
                        nc.tensor.wait_ge(s_dve.h, w2_ev_done[(g, dcg - 2)])
                    first = True
                    for s01 in range(2):
                        wb = (dcg % 2) * 2 + s01
                        for ic in range(NIC):
                            for dc2 in range(2):
                                for half in range(2):
                                    mm = nc.tensor.matmul(
                                        pb[setb + dc2 * 2 + half][:, :],
                                        w2b[wb][:, ic * 256 + dc2 * P:
                                                ic * 256 + (dc2 + 1) * P],
                                        heb[:, (s01 * NIC + ic) * T + half * HB:
                                            (s01 * NIC + ic) * T
                                            + (half + 1) * HB],
                                        start=first,
                                        stop=(s01 == 1 and ic == NIC - 1))
                            first = False
                    w2_pe_done[(g, dcg)] = s_pe.inc(mm)
                    if dcg + 2 < 8:
                        issue_w2(g, dcg + 2, w2_pe_done[(g, dcg)])
                    elif g < 5:
                        issue_w2(g + 1, dcg + 2 - 8, w2_pe_done[(g, dcg)])
                    nc.vector.wait_ge(s_pe.h, w2_pe_done[(g, dcg)])
                    for dc2 in range(2):
                        for half in range(2):
                            dc = dcg * 2 + dc2
                            ev = nc.vector.tensor_tensor(
                                xr[:, dc * T + half * HB:
                                   dc * T + (half + 1) * HB],
                                xr[:, dc * T + half * HB:
                                   dc * T + (half + 1) * HB],
                                pb[setb + dc2 * 2 + half][:, :], ALU.add)
                    w2_ev_done[(g, dcg)] = s_dve.inc(ev)
                if g < 5:
                    gate = w2_ev_done[(g, 7)]
                    build_cB(2 * (g + 1), 0, gate)
                    build_cB(2 * (g + 1) + 1, 1, gate)
            nc.vector.wait_ge(s_dve.h, s_dve.n)
        barrier()

    def attn():
        with contextlib.ExitStack() as ph:
            _u = uid()
            psb = lambda n_, s_, d_: ph.enter_context(nc.psum_tensor(n_ + _u, s_, d_))
            lsb = lambda n_, s_, d_: ph.enter_context(nc.sbuf_tensor(n_ + _u, s_, d_))
            pA = psb("paA", [P, HB], F32)
            pB = psb("paB", [P, HB], F32)
            pZ = [psb("paZ%d" % i2, [P, HB], F32) for i2 in range(2)]
            pD = [psb("paD%d" % i2, [P, HB], F32) for i2 in range(2)]
            pT = [psb("paT%d" % i2, [P, HB], BF16) for i2 in range(2)]
            wkv = [lsb("wkv%d" % i2, [P, NDC * EH], BF16) for i2 in range(2)]
            wst = [lsb("wst%d" % i2, [P, NDC * EH], F32) for i2 in range(2)]
            wob = [lsb("wob%d" % i2, [P, D], BF16) for i2 in range(2)]
            kst = [lsb("kst%d" % i2, [P, T], BF16) for i2 in range(2)]
            kfull = [lsb("kfull%d" % i2, [P, 2 * T], BF16) for i2 in range(2)]
            vE = [lsb("vE%d" % i2, [P, 2 * T], BF16) for i2 in range(2)]
            vt = [lsb("vt%d" % i2, [P, 2 * T], BF16) for i2 in range(2)]
            qsb = [lsb("qsb%d" % i2, [P, T], BF16) for i2 in range(2)]
            expb = [lsb("expb%d" % i2, [P, HB], BF16) for i2 in range(3)]
            ztb = [lsb("ztb%d" % i2, [P, T], BF16) for i2 in range(2)]
            sqs = [lsb("sqs%d" % i2, [P, T], BF16) for i2 in range(2)]
            recb = [lsb("recb%d" % i2, [P, HB], F32) for i2 in range(2)]
            stat = lsb("stat", [P, 16], F32)
            qkmax = lsb("qkmax", [P, 2 * NH], F32)
            negB = lsb("negB", [P, NH], F32)
            def kv_stage(w_dram, bias_s, kv_dram, stat_col0):
                hd0 = s_hd.n
                evac_ms = {}
                square_ms = {}
                statmm_ms = {}

                def emit_square(h2):
                    if h2 >= 2:
                        nc.scalar.wait_ge(s_pe.h, statmm_ms[h2 - 2])
                    aa = nc.scalar.activation(sqs[h2 % 2][:, :],
                                              kst[h2 % 2][:, :], AF.Square)
                    square_ms[h2] = s_act.inc(aa)

                def emit_statmm(h2):
                    nc.tensor.wait_ge(s_act.h, square_ms[h2])
                    if h2 >= 1:
                        nc.tensor.wait_ge(s_dve.h, statred_ms[h2 - 1])
                    for half in range(2):
                        mm2 = nc.tensor.matmul(
                            pZ[half][:, :], ones_b[:, :],
                            sqs[h2 % 2][:, half * HB:(half + 1) * HB],
                            start=True, stop=True)
                    statmm_ms[h2] = s_pe.inc(mm2)
                    nc.vector.wait_ge(s_pe.h, statmm_ms[h2])
                    chain(nc.vector, s_dve,
                          nc.vector.reduce_max(stat[:, 1:2], pZ[0][:, :],
                                               axis=AX))
                    chain(nc.vector, s_dve,
                          nc.vector.reduce_max(stat[:, 2:3], pZ[1][:, :],
                                               axis=AX))
                    chain(nc.vector, s_dve,
                          nc.vector.tensor_tensor(
                              qkmax[:, stat_col0 + h2:stat_col0 + h2 + 1],
                              stat[:, 1:2], stat[:, 2:3], ALU.max))
                    statred_ms[h2] = s_dve.n

                statred_ms = {}
                cast_ms = {}

                def issue_w(h2):
                    wbuf2 = h2 % 2
                    if h2 >= 2:
                        nc.sync.wait_ge(s_pe.h, s_pe.n)
                    i3 = nc.sync.dma_start(wst[wbuf2][:, :],
                                           w_dram[h2 * P:(h2 + 1) * P, :])
                    sw[wbuf2].inc(i3, 16)

                def cast_w(h2):
                    wbuf2 = h2 % 2
                    nc.vector.wait_ge(sw[wbuf2].h, sw[wbuf2].n)
                    if h2 >= 2:
                        nc.vector.wait_ge(s_pe.h, s_pe.n)
                    cc2 = nc.vector.tensor_copy(wkv[wbuf2][:, :],
                                                wst[wbuf2][:, :])
                    cast_ms[h2] = s_dve.inc(cc2)

                issue_w(0)
                issue_w(1)
                cast_w(0)
                for h in range(NH):
                    wbuf = h % 2
                    kb = kst[h % 2]
                    if h + 1 < NH:
                        cast_w(h + 1)
                    nc.tensor.wait_ge(s_dve.h, cast_ms[h])
                    if h >= 1:
                        nc.tensor.wait_ge(s_act.h, evac_ms[h - 1])
                    else:
                        nc.tensor.wait_ge(s_act.h, s_act.n)
                    for half in range(2):
                        pp = pA if half == 0 else pB
                        for dc in range(NDC):
                            mm = nc.tensor.matmul(
                                pp[:, :], wkv[wbuf][:, dc * EH:(dc + 1) * EH],
                                xl[:, dc * T + half * HB:
                                   dc * T + (half + 1) * HB],
                                start=(dc == 0), stop=(dc == NDC - 1))
                        s_pe.inc(mm)
                        nc.scalar.wait_ge(s_pe.h, s_pe.n)
                        if h >= 2 and half == 0:
                            nc.scalar.wait_ge(s_hd.h, hd0 + 16 * (h - 1))
                        aa = nc.scalar.activation(
                            kb[:, half * HB:(half + 1) * HB], pp[:, :],
                            AF.Identity, bias=bias_s[:, h:h + 1])
                        s_act.inc(aa)
                    evac_ms[h] = s_act.n
                    if stat_col0 is not None:
                        emit_square(h)
                    if h + 2 < NH:
                        issue_w(h + 2)
                    nc.sync.wait_ge(s_act.h, evac_ms[h])
                    s_hd.inc(nc.sync.dma_start(
                        kv_dram[h * P:(h + 1) * P, :], kb[:, :]), 16)
                    if stat_col0 is not None and h >= 1:
                        emit_statmm(h - 1)
                if stat_col0 is not None:
                    emit_statmm(NH - 1)

            kv_stage(wk_d, bk_s, kvK_in, NH)
            nc.gpsimd.wait_ge(s_hd.h, s_hd.n)
            s_cc.inc(nc.gpsimd.collective_compute(
                "AllGather", ALU.bypass,
                replica_groups=[[0, 1], [2, 3], [4, 5], [6, 7]],
                ins=[kvK_in.opt()], outs=[kvK_out.opt()]))
            nc.sync.wait_ge(s_dve.h, s_dve.n)
            s_hd.inc(nc.sync.dma_start(kstat_in, qkmax[0:1, NH:2 * NH]), 16)
            nc.gpsimd.wait_ge(s_hd.h, s_hd.n)
            s_cc.inc(nc.gpsimd.collective_compute(
                "AllReduce", ALU.max, replica_groups=[list(range(8))],
                ins=[kstat_in.opt()], outs=[kstat_out.opt()]))
            nc.scalar.wait_ge(s_hd.h, s_hd.n)
            kv_stage(wv_d, bv_s, kvV_in, None)
            nc.gpsimd.wait_ge(s_hd.h, s_hd.n)
            s_cc.inc(nc.gpsimd.collective_compute(
                "AllGather", ALU.bypass,
                replica_groups=[[0, 1], [2, 3], [4, 5], [6, 7]],
                ins=[kvV_in.opt()], outs=[kvV_out.opt()]))
            nc.scalar.wait_ge(s_hd.h, s_hd.n)
            kv_stage(wq_d, bq_s, qstage_d, 0)
            nc.sync.wait_ge(s_cc.h, 2)
            s_hd.inc(nc.sync.dma_start(qkmax[0:1, NH:2 * NH], kstat_out), 16)
            nc.vector.wait_ge(s_hd.h, s_hd.n)
            nc.vector.wait_ge(s_dve.h, s_dve.n)
            chain(nc.vector, s_dve,
                  nc.vector.tensor_tensor(stat[0:1, 0:NH], qkmax[0:1, 0:NH],
                                          qkmax[0:1, NH:2 * NH], ALU.mult))
            chain(nc.vector, s_dve,
                  nc.vector.tensor_scalar_mul(stat[0:1, 0:NH], stat[0:1, 0:NH],
                                              1.0 / 128.0))
            nc.scalar.wait_ge(s_dve.h, s_dve.n)
            chain(nc.scalar, s_act,
                  nc.scalar.activation(recb[0][0:1, 0:NH], stat[0:1, 0:NH],
                                       AF.Ln))
            chain(nc.scalar, s_act,
                  nc.scalar.activation(stat[0:1, 0:NH], recb[0][0:1, 0:NH],
                                       AF.Exp, scale=0.5))
            nc.vector.wait_ge(s_act.h, s_act.n)
            chain(nc.vector, s_dve,
                  nc.vector.tensor_scalar_mul(stat[0:1, 0:NH], stat[0:1, 0:NH],
                                              -1.0))
            nc.tensor.wait_ge(s_dve.h, s_dve.n)
            mm = nc.tensor.matmul(pA[:, 0:NH], ones_f[0:1, :],
                                  stat[0:1, 0:NH], start=True, stop=True)
            s_pe.inc(mm)
            nc.vector.wait_ge(s_pe.h, s_pe.n)
            chain(nc.vector, s_dve,
                  nc.vector.tensor_copy(negB[:, :], pA[:, 0:NH]))

            kv_done = {}
            wo_gate = {}
            pend_wo = []
            zt_done = {}
            exp_done = {}
            msk_done = {}
            avdn_done = {}
            head_pe_done = {}
            q_loaded = {}

            def issue_gathers(h):
                buf = h % 2
                nc.gpsimd.wait_ge(s_cc.h, 1)
                if h >= 2:
                    nc.gpsimd.wait_ge(s_pe.h, head_pe_done[h - 2])
                for s2 in range(2):
                    i2 = nc.gpsimd.indirect_dma_start(
                        kfull[buf][:, s2 * T:(s2 + 1) * T], None,
                        kvK_out[:, :],
                        bass.IndirectOffsetOnAxis(
                            idx_s[:, 2 * h + s2:2 * h + s2 + 1], 0))
                    s_kv.inc(i2, 16)
                nc.gpsimd.wait_ge(s_cc.h, 3)
                for s2 in range(2):
                    i2 = nc.gpsimd.indirect_dma_start(
                        vE[buf][:, s2 * T:(s2 + 1) * T], None,
                        kvV_out[:, :],
                        bass.IndirectOffsetOnAxis(
                            idx_s[:, 2 * h + s2:2 * h + s2 + 1], 0))
                    s_kv.inc(i2, 16)
                kv_done[h] = s_kv.n

            def issue_qload(h):
                buf = h % 2
                if h >= 2:
                    nc.sync.wait_ge(s_pe.h, head_pe_done[h - 2])
                i2 = nc.sync.dma_start(qsb[buf][:, :],
                                       qstage_d[h * P:(h + 1) * P, :])
                q_loaded[h] = s_hd.inc(i2, 16)

            issue_gathers(0)
            issue_gathers(1)
            issue_qload(0)
            issue_qload(1)

            for h in range(NH):
                buf = h % 2
                nc.tensor.wait_ge(s_kv.h, kv_done[h])
                if h >= 2:
                    nc.tensor.wait_ge(s_pe.h, head_pe_done[h - 2])
                act_vt0 = s_act.n
                for kt in range(16):
                    pt = pT[kt % 2]
                    if kt >= 2:
                        nc.tensor.wait_ge(s_act.h, act_vt0 + kt - 1)
                    s_pe.inc(nc.tensor.transpose(
                        pt[:, 0:P], vE[buf][:, kt * P:(kt + 1) * P],
                        ident_b[:, :]))
                    nc.scalar.wait_ge(s_pe.h, s_pe.n)
                    s_act.inc(nc.scalar.activation(
                        vt[buf][:, kt * P:(kt + 1) * P], pt[:, 0:P], AF.Copy))

                if pend_wo:
                    g2p, ztg, fn = pend_wo.pop(0)
                    fn(g2p, ztg)

                nc.tensor.wait_ge(s_hd.h, q_loaded[h])
                def emit_sc(it):
                    kt, half = divmod(it, 2)
                    psc = pA if it % 2 == 0 else pB
                    if it >= 2:
                        nc.tensor.wait_ge(s_act.h, exp_done[(h, it - 2)])
                    else:
                        nc.tensor.wait_ge(s_dve.h, s_dve.n)
                    mm = nc.tensor.matmul(
                        psc[:, :], kfull[buf][:, kt * P:(kt + 1) * P],
                        qsb[buf][:, half * HB:(half + 1) * HB],
                        start=True, stop=False)
                    g0 = 2048 - kt * P + half * HB
                    mm = nc.tensor.matmul(psc[:, :], ident_b[:, :],
                                          bigm[:, g0:g0 + HB],
                                          start=False, stop=True)
                    pe_n = s_pe.inc(mm)
                    nc.scalar.wait_ge(s_pe.h, pe_n)
                    if it >= 3:
                        nc.scalar.wait_ge(s_pe.h, avdn_done[(h, it - 3)])
                    a2 = nc.scalar.activation(expb[it % 3][:, :], psc[:, :],
                                              AF.Exp, scale=SCALE_QK,
                                              bias=negB[:, h:h + 1])
                    exp_done[(h, it)] = s_act.inc(a2)

                def emit_av(it):
                    kt, half = divmod(it, 2)
                    nc.tensor.wait_ge(s_act.h, exp_done[(h, it)])
                    mm = nc.tensor.matmul(
                        pZ[half][:, :], vt[buf][:, kt * P:(kt + 1) * P],
                        expb[it % 3][:, :],
                        start=(kt == 0), stop=(kt == 15))
                    mm2 = nc.tensor.matmul(
                        pD[half][:, :], ones_b[:, :], expb[it % 3][:, :],
                        start=(kt == 0), stop=(kt == 15))
                    avdn_done[(h, it)] = s_pe.inc(mm2)

                for it in range(32):
                    emit_sc(it)
                    if it >= 2:
                        emit_av(it - 2)
                emit_av(30)
                emit_av(31)
                head_pe_done[h] = s_pe.n
                if h + 2 < NH:
                    issue_gathers(h + 2)
                    issue_qload(h + 2)
                nc.vector.wait_ge(s_pe.h, head_pe_done[h])
                for half in range(2):
                    chain(nc.vector, s_dve,
                          nc.vector.reciprocal(recb[half][:, :],
                                               pD[half][:, :]))
                    s_dve.inc(nc.vector.tensor_tensor(
                        ztb[h % 2][:, half * HB:(half + 1) * HB],
                        pZ[half][:, :], recb[half][:, :], ALU.mult))
                zt_done[h] = s_dve.n

                if h % 2 == 0:
                    g2n = h // 2
                    for hh in range(2):
                        if g2n >= 1:
                            nc.gpsimd.wait_ge(s_pe.h, wo_gate[g2n - 1])
                        i2 = nc.gpsimd.dma_start(wob[hh][:, :],
                                                 wo_d[g2n * 2 + hh])
                        sw[4 + hh].inc(i2, 16)
                def emit_wo(g2, zt_gate):
                    nc.tensor.wait_ge(s_dve.h, zt_gate)
                    nc.tensor.wait_ge(sw[4].h, sw[4].n)
                    nc.tensor.wait_ge(sw[5].h, sw[5].n)
                    rot = [pA, pB, pZ[0], pZ[1]]
                    ev_ms = {}
                    for dc in range(NDC):
                        for half in range(2):
                            pr2 = rot[(dc * 2 + half) % 4]
                            if dc * 2 + half >= 4:
                                nc.tensor.wait_ge(s_dve.h,
                                                  ev_ms[dc * 2 + half - 4])
                            for hh in range(2):
                                mm = nc.tensor.matmul(
                                    pr2[:, :],
                                    wob[hh][:, dc * P:(dc + 1) * P],
                                    ztb[hh][:, half * HB:(half + 1) * HB],
                                    start=(hh == 0), stop=(hh == 1))
                            s_pe.inc(mm)
                            nc.vector.wait_ge(s_pe.h, s_pe.n)
                            if g2 == 0:
                                ev = nc.vector.scalar_tensor_tensor(
                                    xr[:, dc * T + half * HB:
                                       dc * T + (half + 1) * HB],
                                    pr2[:, :], bo_s[:, dc:dc + 1],
                                    xr[:, dc * T + half * HB:
                                       dc * T + (half + 1) * HB],
                                    ALU.add, ALU.add)
                            else:
                                ev = nc.vector.tensor_tensor(
                                    xr[:, dc * T + half * HB:
                                       dc * T + (half + 1) * HB],
                                    xr[:, dc * T + half * HB:
                                       dc * T + (half + 1) * HB],
                                    pr2[:, :], ALU.add)
                            ev_ms[dc * 2 + half] = s_dve.inc(ev)
                    wo_gate[g2] = s_pe.n

                if h % 2 == 1:
                    pend_wo.append((h // 2, zt_done[h], emit_wo))
            g2l, ztg, fn = pend_wo.pop(0)
            fn(g2l, ztg)
            nc.vector.wait_ge(s_dve.h, s_dve.n)
        barrier()

    layer_norm()
    ffn_call()
    if n_sub >= 2:
        layer_norm()
        attn()
    if n_sub >= 3:
        layer_norm()
        ffn_call()

    for s in sems.values():
        if s.n:
            nc.sync.wait_ge(s.h, s.n)
    out = nc.sync.dma_start(
        y_d.rearrange("(dc p) t -> p dc t", p=P),
        xr[:, :].rearrange("p (dc t) -> p dc t", t=T))
    s_out.inc(out, 16)
    nc.sync.wait_ge(s_out.h, s_out.n)
    st.close()
    return nc


def _stage_inputs(inputs):
    f32 = np.float32
    asf = lambda a: np.ascontiguousarray(np.asarray(a, dtype=f32))
    x = asf(inputs["L0_attn_x"])
    w1T = asf(inputs["L0_ffn_w1"]).transpose(0, 2, 1)
    sguT = asf(inputs["L0_ffn_shared_gate_up_w"]).T

    def pack_w1(col0):
        a = w1T[:, :, col0:col0 + IEX].reshape(8, NDC, P, NIC, P)
        a = a.transpose(0, 3, 2, 1, 4).reshape(8 * NIC * P, NDC * P)
        b = sguT[:, col0 // IEX * ISH:col0 // IEX * ISH + ISH]
        b = b.reshape(NDC, P, 4, NIC, P).transpose(2, 3, 1, 0, 4)
        return np.ascontiguousarray(
            np.concatenate([a, b.reshape(4 * NIC * P, NDC * P)], axis=0))

    w2T = asf(inputs["L0_ffn_w2"]).transpose(0, 2, 1)
    a2 = w2T.reshape(8, NIC, P, 8, 256).transpose(0, 3, 2, 1, 4)
    sdT = asf(inputs["L0_ffn_shared_down_w"]).T
    b2 = sdT.reshape(4, NIC, P, 8, 256).transpose(0, 3, 2, 1, 4)
    w2p = np.ascontiguousarray(np.concatenate(
        [a2.reshape(8 * 8 * P, NIC * 256), b2.reshape(4 * 8 * P, NIC * 256)],
        axis=0))

    def pack_qkv(w):
        a = asf(w).reshape(NH, NDC, P, EH).transpose(0, 2, 1, 3)
        return np.ascontiguousarray(a.reshape(NH * P, NDC * EH))

    shared = {
        "w1gp": pack_w1(0),
        "w1up": pack_w1(IEX),
        "w2p": w2p,
        "gw9T": np.ascontiguousarray(
            np.concatenate([asf(inputs["L0_ffn_gate_w"]),
                            asf(inputs["L0_ffn_shared_gate_w"]),
                            np.zeros((7, D), f32)], axis=0).T),
        "wqp": pack_qkv(inputs["L0_attn_W_Q"]),
        "wkp": pack_qkv(inputs["L0_attn_W_K"]),
        "wvp": pack_qkv(inputs["L0_attn_W_V"]),
        "wo": asf(inputs["L0_attn_W_O"]),
        "bqT": np.ascontiguousarray(asf(inputs["L0_attn_b_Q"]).T),
        "bkT": np.ascontiguousarray(asf(inputs["L0_attn_b_K"]).T),
        "bvT": np.ascontiguousarray(asf(inputs["L0_attn_b_V"]).T),
        "bo": np.ascontiguousarray(asf(inputs["L0_attn_b_O"]).reshape(NDC, P).T),
    }
    per_core = []
    for r in range(8):
        b, hh = r // 2, r % 2
        shard = np.ascontiguousarray(x[b, hh * T:(hh + 1) * T, :].T)
        idx = np.empty((P, 2 * NH), np.int32)
        for h in range(NH):
            idx[:, 2 * h + 0] = h * P + np.arange(P)
            idx[:, 2 * h + 1] = D + h * P + np.arange(P)
        m = {"xT": shard, "qbase": np.array([[hh * T]], f32), "kvidx": idx}
        m.update(shared)
        per_core.append(m)
    return per_core


def kernel(**inputs):
    per_core = _stage_inputs(inputs)
    nc = build_program(n_sub=3)
    res = run_bass_kernel_spmd(nc, per_core, core_ids=list(range(8)))
    out = np.empty((4, 2048, D), np.float32)
    for r in range(8):
        b, hh = r // 2, r % 2
        out[b, hh * T:(hh + 1) * T, :] = np.asarray(res.results[r]["y"]).T
    return out


# version 07
# speedup vs baseline: 1.0401x; 1.0209x over previous
import contextlib
import numpy as np
import concourse.bass as bass
import concourse.mybir as mybir
from concourse.bass_utils import run_bass_kernel_spmd

F32 = mybir.dt.float32
BF16 = mybir.dt.bfloat16
I32 = mybir.dt.int32
AF = mybir.ActivationFunctionType
ALU = mybir.AluOpType
AX = mybir.AxisListType.X

P = 128
D = 2048
NDC = 16
T = 1024
NH = 16
EH = 128
IEX = 1408
NIC = 11
NPE = 12
ISH = 5632
SCALE_QK = float(1.0 / np.sqrt(128.0))
NEG_BIG = -100000.0
LN_EPS = 1e-5
HB = 512


class Sem:
    def __init__(self, st, nc, name):
        self.h = st.enter_context(nc.semaphore(name))
        self.n = 0

    def inc(self, instr, k=1):
        instr.then_inc(self.h, k)
        self.n += k
        return self.n


def build_program(n_sub=3, debug=False):
    nc = bass.Bass("TRN2", target_bir_lowering=False)
    uid_ctr = [0]
    def uid():
        uid_ctr[0] += 1
        return "_u%d" % uid_ctr[0]

    xT_d = nc.dram_tensor("xT", [D, T], F32, kind="ExternalInput").ap()
    qb_d = nc.dram_tensor("qbase", [1, 1], F32, kind="ExternalInput").ap()
    idx_d = nc.dram_tensor("kvidx", [P, 2 * NH], I32, kind="ExternalInput").ap()
    w1gp_d = nc.dram_tensor("w1gp", [NPE * NIC * P, NDC * P], F32,
                            kind="ExternalInput").ap()
    w1up_d = nc.dram_tensor("w1up", [NPE * NIC * P, NDC * P], F32,
                            kind="ExternalInput").ap()
    w2p_d = nc.dram_tensor("w2p", [NPE * 8 * P, NIC * 256], F32,
                           kind="ExternalInput").ap()
    gw9T_d = nc.dram_tensor("gw9T", [D, 16], F32, kind="ExternalInput").ap()
    wq_d = nc.dram_tensor("wqp", [NH * P, NDC * EH], F32, kind="ExternalInput").ap()
    wk_d = nc.dram_tensor("wkp", [NH * P, NDC * EH], F32, kind="ExternalInput").ap()
    wv_d = nc.dram_tensor("wvp", [NH * P, NDC * EH], F32, kind="ExternalInput").ap()
    wo_d = nc.dram_tensor("wo", [NH, EH, D], F32, kind="ExternalInput").ap()
    bqT_d = nc.dram_tensor("bqT", [P, NH], F32, kind="ExternalInput").ap()
    bkT_d = nc.dram_tensor("bkT", [P, NH], F32, kind="ExternalInput").ap()
    bvT_d = nc.dram_tensor("bvT", [P, NH], F32, kind="ExternalInput").ap()
    bo_d = nc.dram_tensor("bo", [P, NDC], F32, kind="ExternalInput").ap()
    y_d = nc.dram_tensor("y", [D, T], F32, kind="ExternalOutput").ap()
    dbg_d = (nc.dram_tensor("dbg", [P, 4096], F32, kind="ExternalOutput").ap()
             if debug else None)
    dbg2_d = (nc.dram_tensor("dbg2", [NH * P, T], F32, kind="ExternalOutput").ap()
              if debug else None)

    kvK_in = nc.dram_tensor("kvK_in", [D, T], BF16).ap()
    kvV_in = nc.dram_tensor("kvV_in", [D, T], BF16).ap()
    kvK_out = nc.dram_tensor("kvK_out", [2 * D, T], BF16).ap()
    kvV_out = nc.dram_tensor("kvV_out", [2 * D, T], BF16).ap()
    qstage_d = nc.dram_tensor("qstage", [D, T], BF16).ap()
    kstat_in = nc.dram_tensor("kstat_in", [1, NH], F32).ap()
    kstat_out = nc.dram_tensor("kstat_out", [1, NH], F32).ap()

    st = contextlib.ExitStack()
    sb = lambda name, shape, dt: st.enter_context(nc.sbuf_tensor(name, shape, dt))

    xr = sb("xr", [P, NDC * T], F32)
    xl = sb("xl", [P, NDC * T], BF16)
    ones_b = sb("ones_b", [P, P], BF16)
    ones_f = sb("ones_f", [1, P], F32)
    half_f = sb("half_f", [1, P], F32)
    half_b = sb("half_b", [1, P], BF16)
    ident_f = sb("ident_f", [P, P], F32)
    ident_b = sb("ident_b", [P, P], BF16)
    iota_p = sb("iota_p", [P, 1], F32)
    qbB = sb("qbB", [P, 1], F32)
    gw9 = sb("gw9", [P, NDC * 16], BF16)
    bq_s = sb("bq_s", [P, NH], F32)
    bk_s = sb("bk_s", [P, NH], F32)
    bv_s = sb("bv_s", [P, NH], F32)
    bo_s = sb("bo_s", [P, NDC], F32)
    qb_s = sb("qb_s", [1, 1], F32)
    idx_s = sb("idx_s", [P, 2 * NH], I32)
    iota_sml = sb("iota_sml", [P, P], F32)
    bigm = sb("bigm", [P, 3072], BF16)

    sems = {}
    for name in ["pe", "dve", "act", "pool", "hd", "cc", "out", "kv",
                 "w0", "w1", "w2", "w3", "w4", "w5"]:
        sems[name] = Sem(st, nc, "s_" + name)
    s_pe, s_dve, s_act = sems["pe"], sems["dve"], sems["act"]
    s_pool, s_hd, s_cc = sems["pool"], sems["hd"], sems["cc"]
    s_out, s_kv = sems["out"], sems["kv"]
    sw = [sems["w%d" % i] for i in range(6)]

    ENGS = [nc.tensor, nc.vector, nc.scalar, nc.gpsimd, nc.sync]

    def barrier():
        for eng in ENGS:
            for s in sems.values():
                if s.n:
                    eng.wait_ge(s.h, s.n)

    def chain(eng, sem, instr, k=1):
        sem.inc(instr, k)
        eng.wait_ge(sem.h, sem.n)

    i = nc.sync.dma_start(
        xr[:, :].rearrange("p (dc t) -> p dc t", t=T),
        xT_d.rearrange("(dc p) t -> p dc t", p=P))
    s_hd.inc(i, 16)
    for dst, src in [(qb_s[:, :], qb_d), (idx_s[:, :], idx_d),
                     (bq_s[:, :], bqT_d), (bk_s[:, :], bkT_d),
                     (bv_s[:, :], bvT_d)]:
        s_hd.inc(nc.sync.dma_start(dst, src), 16)
    s_pool.inc(nc.gpsimd.dma_start(bo_s[:, :], bo_d), 16)
    s_pool.inc(nc.gpsimd.dma_start(
        gw9[:, :].rearrange("p (dc w) -> p dc w", w=16),
        gw9T_d.rearrange("(dc p) w -> p dc w", p=P)), 16)
    chain(nc.gpsimd, s_pool,
          nc.gpsimd.iota(iota_p[:, :], pattern=[[0, 1]], base=0,
                         channel_multiplier=1,
                         allow_small_or_imprecise_dtypes=True))
    chain(nc.gpsimd, s_pool,
          nc.gpsimd.iota(iota_sml[:, :], pattern=[[1, P]], base=0,
                         channel_multiplier=0,
                         allow_small_or_imprecise_dtypes=True))
    nc.vector.memset(ones_b[:, :], 1.0)
    nc.vector.memset(ones_f[:, :], 1.0)
    nc.vector.memset(half_f[:, :], 0.5)
    nc.vector.memset(half_b[:, :], 0.5)
    nc.vector.wait_ge(s_pool.h, s_pool.n)
    chain(nc.vector, s_dve,
          nc.vector.tensor_scalar(ident_f[:, :], iota_sml[:, :], iota_p[:, 0:1],
                                  None, ALU.is_equal))
    chain(nc.vector, s_dve, nc.vector.tensor_copy(ident_b[:, :], ident_f[:, :]))
    with nc.psum_tensor("pqb2", [P, 512], F32) as pqb:
        nc.tensor.wait_ge(s_hd.h, s_hd.n)
        nc.tensor.wait_ge(s_dve.h, s_dve.n)
        mm = nc.tensor.matmul(pqb[:, 0:1], ones_f[0:1, :], qb_s[0:1, 0:1],
                              start=True, stop=True)
        s_pe.inc(mm)
        nc.vector.wait_ge(s_pe.h, s_pe.n)
        chain(nc.vector, s_dve, nc.vector.tensor_copy(qbB[:, :], pqb[:, 0:1]))
    with nc.sbuf_tensor("iota3k_tmp", [P, 3072], F32) as iota3k, \
         nc.sbuf_tensor("tcol_tmp", [P, 1], F32) as tcol:
        chain(nc.gpsimd, s_pool,
              nc.gpsimd.iota(iota3k[:, :], pattern=[[1, 3072]], base=-2048,
                             channel_multiplier=0,
                             allow_small_or_imprecise_dtypes=True))
        nc.vector.wait_ge(s_pool.h, s_pool.n)
        nc.vector.wait_ge(s_dve.h, s_dve.n)
        nc.vector.wait_ge(s_pe.h, s_pe.n)
        chain(nc.vector, s_dve,
              nc.vector.tensor_tensor(tcol[:, :], iota_p[:, :], qbB[:, :],
                                      ALU.subtract))
        chain(nc.vector, s_dve,
              nc.vector.tensor_scalar(bigm[:, :], iota3k[:, :], tcol[:, 0:1],
                                      NEG_BIG, ALU.is_lt, ALU.mult))
    barrier()

    def layer_norm():
        with contextlib.ExitStack() as ph:
            _u = uid()
            psb = lambda n_, s_, d_: ph.enter_context(nc.psum_tensor(n_ + _u, s_, d_))
            lsb = lambda n_, s_, d_: ph.enter_context(nc.sbuf_tensor(n_ + _u, s_, d_))
            p_s = [psb("pls%d" % i2, [P, HB], F32) for i2 in range(2)]
            p_q = [psb("plq%d" % i2, [P, HB], F32) for i2 in range(2)]
            xb_t = [lsb("lnxb%d" % i2, [P, T], BF16) for i2 in range(2)]
            sq_t = [lsb("lnsq%d" % i2, [P, T], BF16) for i2 in range(2)]
            mB = lsb("lnmB", [P, T], F32)
            v1 = lsb("lnv1", [P, T], F32)
            rstd = lsb("lnrstd", [P, T], F32)
            tt32 = [lsb("lnt%d" % i2, [P, T], F32) for i2 in range(4)]

            dve0, act0, pe0 = s_dve.n, s_act.n, s_pe.n
            for dc in range(NDC):
                b = dc % 2
                if dc >= 2:
                    nc.vector.wait_ge(s_pe.h, pe0 + 4 * (dc - 1))
                    nc.scalar.wait_ge(s_pe.h, pe0 + 4 * (dc - 1))
                s_dve.inc(nc.vector.tensor_copy(xb_t[b][:, :],
                                                xr[:, dc * T:(dc + 1) * T]))
                s_act.inc(nc.scalar.activation(sq_t[b][:, :],
                                               xr[:, dc * T:(dc + 1) * T],
                                               AF.Square))
                nc.tensor.wait_ge(s_dve.h, dve0 + dc + 1)
                nc.tensor.wait_ge(s_act.h, act0 + dc + 1)
                for half in range(2):
                    s_pe.inc(nc.tensor.matmul(
                        p_s[half][:, :], ones_b[:, :],
                        xb_t[b][:, half * HB:(half + 1) * HB],
                        start=(dc == 0), stop=(dc == NDC - 1)))
                    s_pe.inc(nc.tensor.matmul(
                        p_q[half][:, :], ones_b[:, :],
                        sq_t[b][:, half * HB:(half + 1) * HB],
                        start=(dc == 0), stop=(dc == NDC - 1)))
            nc.scalar.wait_ge(s_pe.h, s_pe.n)
            for half in range(2):
                sl = slice(half * HB, (half + 1) * HB)
                s_act.inc(nc.scalar.activation(mB[:, sl], p_s[half][:, :],
                                               AF.Copy, scale=1.0 / D))
                s_act.inc(nc.scalar.activation(v1[:, sl], p_q[half][:, :],
                                               AF.Copy, scale=1.0 / D))
            nc.vector.wait_ge(s_act.h, s_act.n)
            chain(nc.vector, s_dve,
                  nc.vector.tensor_tensor(tt32[0][:, :], mB[:, :], mB[:, :],
                                          ALU.mult))
            chain(nc.vector, s_dve,
                  nc.vector.tensor_tensor(v1[:, :], v1[:, :], tt32[0][:, :],
                                          ALU.subtract))
            chain(nc.vector, s_dve,
                  nc.vector.tensor_scalar_add(v1[:, :], v1[:, :], LN_EPS))
            nc.scalar.wait_ge(s_dve.h, s_dve.n)
            chain(nc.scalar, s_act,
                  nc.scalar.activation(tt32[1][:, :], v1[:, :], AF.Ln))
            chain(nc.scalar, s_act,
                  nc.scalar.activation(rstd[:, :], tt32[1][:, :], AF.Exp,
                                       scale=-0.5))
            nc.vector.wait_ge(s_act.h, s_act.n)
            nc.gpsimd.wait_ge(s_act.h, s_act.n)
            dcnt = {}
            pcnt = {}
            for dc in range(NDC):
                b = dc % 2
                if dc < 10:
                    if dc >= 2:
                        nc.vector.wait_ge(s_dve.h, dcnt[dc - 2])
                    s_dve.inc(nc.vector.tensor_tensor(
                        tt32[b][:, :], xr[:, dc * T:(dc + 1) * T],
                        mB[:, :], ALU.subtract))
                    nc.vector.wait_ge(s_dve.h, s_dve.n)
                    s_dve.inc(nc.vector.tensor_tensor(
                        xl[:, dc * T:(dc + 1) * T], tt32[b][:, :],
                        rstd[:, :], ALU.mult))
                    dcnt[dc] = s_dve.n
                else:
                    if dc >= 12:
                        nc.gpsimd.wait_ge(s_pool.h, pcnt[dc - 2])
                    s_pool.inc(nc.gpsimd.tensor_tensor(
                        tt32[2 + b][:, :], xr[:, dc * T:(dc + 1) * T],
                        mB[:, :], ALU.subtract))
                    nc.gpsimd.wait_ge(s_pool.h, s_pool.n)
                    s_pool.inc(nc.gpsimd.tensor_tensor(
                        xl[:, dc * T:(dc + 1) * T], tt32[2 + b][:, :],
                        rstd[:, :], ALU.mult))
                    pcnt[dc] = s_pool.n
            nc.vector.wait_ge(s_dve.h, s_dve.n)
            nc.vector.wait_ge(s_pool.h, s_pool.n)
        barrier()

    def w1_srcs(e, ic):
        r0 = (e * NIC + ic) * P
        return w1gp_d[r0:r0 + P, :], w1up_d[r0:r0 + P, :]

    def w2_src(e, dcg):
        r0 = (e * 8 + dcg) * P
        return w2p_d[r0:r0 + P, :]

    def ffn_call():
        with contextlib.ExitStack() as ph:
            _u = uid()
            psb = lambda n_, s_, d_: ph.enter_context(nc.psum_tensor(n_ + _u, s_, d_))
            lsb = lambda n_, s_, d_: ph.enter_context(nc.sbuf_tensor(n_ + _u, s_, d_))
            pb = [psb("pf%d" % i2, [P, HB], F32) for i2 in range(8)]
            w1g = [lsb("w1g%d" % i2, [P, NDC * P], BF16) for i2 in range(2)]
            w1u = [lsb("w1u%d" % i2, [P, NDC * P], BF16) for i2 in range(2)]
            w2b = [lsb("w2b%d" % i2, [P, NIC * 256], BF16) for i2 in range(4)]
            heb = lsb("heb", [P, 2 * NIC * T], BF16)
            cBb = [lsb("cB%d" % i2, [P, T], F32) for i2 in range(2)]
            sgb = [lsb("sg%d" % i2, [P, HB], BF16) for i2 in range(2)]
            tmpb = [lsb("tmp%d" % i2, [P, HB], BF16) for i2 in range(2)]
            lg9 = lsb("lg9", [P, 8 * 9], F32)
            cm9 = lsb("cm9", [P, 8 * 9], F32)
            crow = lsb("crow", [1, T], BF16)
            rsc = lsb("rsc", [P, 8 * 24], F32)

            act_r0 = s_act.n
            for tt in range(8):
                pr = pb[tt % 2]
                if tt >= 2:
                    nc.tensor.wait_ge(s_act.h, act_r0 + (tt - 1))
                for dc in range(NDC):
                    mm = nc.tensor.matmul(
                        pr[:, 0:9],
                        xl[:, dc * T + tt * P:dc * T + (tt + 1) * P],
                        gw9[:, dc * 16:dc * 16 + 9],
                        start=(dc == 0), stop=(dc == NDC - 1))
                s_pe.inc(mm)
                nc.scalar.wait_ge(s_pe.h, s_pe.n)
                s_act.inc(nc.scalar.activation(lg9[:, tt * 9:(tt + 1) * 9],
                                               pr[:, 0:9], AF.Copy))
            nc.vector.wait_ge(s_act.h, s_act.n)
            for tt in range(8):
                lg = lg9[:, tt * 9:tt * 9 + 8]
                sc = rsc[:, tt * 24:(tt + 1) * 24]
                chain(nc.vector, s_dve,
                      nc.vector.reduce_max(sc[:, 0:1], lg, axis=AX))
                chain(nc.vector, s_dve,
                      nc.vector.tensor_scalar_mul(sc[:, 1:2], sc[:, 0:1], -1.0))
                nc.scalar.wait_ge(s_dve.h, s_dve.n)
                chain(nc.scalar, s_act,
                      nc.scalar.activation(sc[:, 2:10], lg, AF.Exp,
                                           bias=sc[:, 1:2]))
                nc.vector.wait_ge(s_act.h, s_act.n)
                chain(nc.vector, s_dve,
                      nc.vector.reduce_sum(sc[:, 10:11], sc[:, 2:10], axis=AX))
                chain(nc.vector, s_dve,
                      nc.vector.reciprocal(sc[:, 11:12], sc[:, 10:11]))
                chain(nc.vector, s_dve,
                      nc.vector.tensor_scalar(sc[:, 2:10], sc[:, 2:10],
                                              sc[:, 11:12], None, ALU.mult))
                chain(nc.vector, s_dve,
                      nc.vector.reduce_max(sc[:, 12:13], sc[:, 2:10], axis=AX))
                chain(nc.vector, s_dve,
                      nc.vector.tensor_scalar(sc[:, 16:24], sc[:, 2:10],
                                              sc[:, 12:13], None, ALU.is_lt))
                chain(nc.vector, s_dve,
                      nc.vector.tensor_tensor(sc[:, 16:24], sc[:, 2:10],
                                              sc[:, 16:24], ALU.mult))
                chain(nc.vector, s_dve,
                      nc.vector.reduce_max(sc[:, 13:14], sc[:, 16:24], axis=AX))
                chain(nc.vector, s_dve,
                      nc.vector.tensor_scalar(sc[:, 16:24], sc[:, 2:10],
                                              sc[:, 13:14], None, ALU.is_ge))
                chain(nc.vector, s_dve,
                      nc.vector.tensor_tensor(sc[:, 14:15], sc[:, 12:13],
                                              sc[:, 13:14], ALU.add))
                chain(nc.vector, s_dve,
                      nc.vector.reciprocal(sc[:, 15:16], sc[:, 14:15]))
                chain(nc.vector, s_dve,
                      nc.vector.tensor_tensor(sc[:, 16:24], sc[:, 2:10],
                                              sc[:, 16:24], ALU.mult))
                chain(nc.vector, s_dve,
                      nc.vector.tensor_scalar(cm9[:, tt * 9:tt * 9 + 8],
                                              sc[:, 16:24], sc[:, 15:16], None,
                                              ALU.mult))
                nc.scalar.wait_ge(s_dve.h, s_dve.n)
                chain(nc.scalar, s_act,
                      nc.scalar.activation(cm9[:, tt * 9 + 8:tt * 9 + 9],
                                           lg9[:, tt * 9 + 8:tt * 9 + 9],
                                           AF.Sigmoid))
            nc.vector.wait_ge(s_act.h, s_act.n)

            def build_cB(e, slot, gate_dve):
                ce = e if e < 8 else 8
                nc.tensor.wait_ge(s_dve.h, gate_dve)
                nc.tensor.wait_ge(s_act.h, s_act.n)
                d_base = s_dve.n
                for tt in range(8):
                    pt = pb[6 + tt % 2]
                    if tt >= 2:
                        nc.tensor.wait_ge(s_dve.h, d_base + tt - 1)
                    tr = nc.tensor.transpose(
                        pt[0:1, 0:P], cm9[:, tt * 9 + ce:tt * 9 + ce + 1],
                        ident_f[:, :])
                    s_pe.inc(tr)
                    nc.vector.wait_ge(s_pe.h, s_pe.n)
                    s_dve.inc(nc.vector.tensor_copy(
                        crow[0:1, tt * P:(tt + 1) * P], pt[0:1, 0:P]))
                nc.tensor.wait_ge(s_dve.h, s_dve.n)
                for half in range(2):
                    mm = nc.tensor.matmul(pb[6 + half][:, :], half_b[0:1, :],
                                          crow[0:1, half * HB:(half + 1) * HB],
                                          start=True, stop=True)
                    s_pe.inc(mm)
                nc.vector.wait_ge(s_pe.h, s_pe.n)
                nc.vector.wait_ge(s_pool.h, s_pool.n)
                for half in range(2):
                    s_dve.inc(nc.vector.tensor_copy(
                        cBb[slot][:, half * HB:(half + 1) * HB],
                        pb[6 + half][:, :]))
                return s_dve.n

            w1_pe_done = {}
            silu_done = {}
            tmp_done = {}
            he_done = {}
            w2_pe_done = {}
            w2_ev_done = {}
            wo_gate = {}

            def issue_w1(e, ic, gate_pe):
                buf = (e * NIC + ic) % 2
                g_src, u_src = w1_srcs(e, ic)
                if gate_pe is not None:
                    nc.gpsimd.wait_ge(s_pe.h, gate_pe)
                i1 = nc.gpsimd.dma_start(w1g[buf][:, :], g_src)
                sw[buf].inc(i1, 16)
                i2 = nc.gpsimd.dma_start(w1u[buf][:, :], u_src)
                sw[buf].inc(i2, 16)

            def issue_w2(g, dcg, gate_pe):
                for s01 in range(2):
                    buf = (dcg % 2) * 2 + s01
                    if gate_pe is not None:
                        nc.gpsimd.wait_ge(s_pe.h, gate_pe)
                    i1 = nc.gpsimd.dma_start(w2b[buf][:, :],
                                             w2_src(2 * g + s01, dcg))
                    sw[2 + buf].inc(i1, 16)

            issue_w1(0, 0, None)
            issue_w1(0, 1, None)
            build_cB(0, 0, s_dve.n)
            build_cB(1, 1, s_dve.n)
            issue_w2(0, 0, None)
            issue_w2(0, 1, None)

            step = 0
            for g in range(6):
                for s01 in range(2):
                    e = 2 * g + s01
                    for ic in range(NIC):
                        gbuf = (e * NIC + ic) % 2
                        nc.tensor.wait_ge(sw[gbuf].h, sw[gbuf].n)
                        for half in range(2):
                            gps = pb[step % 2]
                            ups = pb[2 + step % 2]
                            if step >= 2:
                                nc.tensor.wait_ge(s_act.h, silu_done[step - 2])
                                nc.tensor.wait_ge(s_dve.h, tmp_done[step - 2])
                            for dc in range(NDC):
                                mm = nc.tensor.matmul(
                                    gps[:, :],
                                    w1g[gbuf][:, dc * P:(dc + 1) * P],
                                    xl[:, dc * T + half * HB:
                                       dc * T + (half + 1) * HB],
                                    start=(dc == 0), stop=(dc == NDC - 1))
                            for dc in range(NDC):
                                mm = nc.tensor.matmul(
                                    ups[:, :],
                                    w1u[gbuf][:, dc * P:(dc + 1) * P],
                                    xl[:, dc * T + half * HB:
                                       dc * T + (half + 1) * HB],
                                    start=(dc == 0), stop=(dc == NDC - 1))
                            s_pe.inc(mm)
                            w1_pe_done[(e, ic)] = s_pe.n
                            nc.scalar.wait_ge(s_pe.h, s_pe.n)
                            if step >= 2:
                                nc.scalar.wait_ge(s_dve.h, tmp_done[step - 2])
                            a = nc.scalar.activation(sgb[step % 2][:, :],
                                                     gps[:, :], AF.Silu)
                            silu_done[step] = s_act.inc(a)
                            nc.vector.wait_ge(s_act.h, silu_done[step])
                            if step >= 2:
                                nc.vector.wait_ge(s_pool.h, he_done[step - 2])
                            v = nc.vector.scalar_tensor_tensor(
                                tmpb[step % 2][:, :], sgb[step % 2][:, :], 1.0,
                                ups[:, :], ALU.mult, ALU.mult)
                            tmp_done[step] = s_dve.inc(v)
                            nc.gpsimd.wait_ge(s_dve.h, tmp_done[step])
                            hs = (s01 * NIC + ic) * T + half * HB
                            pl = nc.gpsimd.tensor_tensor(
                                heb[:, hs:hs + HB], tmpb[step % 2][:, :],
                                cBb[s01][:, half * HB:(half + 1) * HB],
                                ALU.mult)
                            he_done[step] = s_pool.inc(pl)
                            step += 1
                        nx = e * NIC + ic + 2
                        if nx < NPE * NIC:
                            ne, nic2 = divmod(nx, NIC)
                            issue_w1(ne, nic2, w1_pe_done[(e, ic)])
                last_he = he_done[step - 1]
                for dcg in range(8):
                    setb = (dcg % 2) * 4
                    nc.tensor.wait_ge(s_pool.h, last_he)
                    for s01 in range(2):
                        wb = (dcg % 2) * 2 + s01
                        nc.tensor.wait_ge(sw[2 + wb].h, sw[2 + wb].n)
                    if dcg >= 2:
                        nc.tensor.wait_ge(s_dve.h, w2_ev_done[(g, dcg - 2)])
                    first = True
                    for s01 in range(2):
                        wb = (dcg % 2) * 2 + s01
                        for ic in range(NIC):
                            for dc2 in range(2):
                                for half in range(2):
                                    mm = nc.tensor.matmul(
                                        pb[setb + dc2 * 2 + half][:, :],
                                        w2b[wb][:, ic * 256 + dc2 * P:
                                                ic * 256 + (dc2 + 1) * P],
                                        heb[:, (s01 * NIC + ic) * T + half * HB:
                                            (s01 * NIC + ic) * T
                                            + (half + 1) * HB],
                                        start=first,
                                        stop=(s01 == 1 and ic == NIC - 1))
                            first = False
                    w2_pe_done[(g, dcg)] = s_pe.inc(mm)
                    if dcg + 2 < 8:
                        issue_w2(g, dcg + 2, w2_pe_done[(g, dcg)])
                    elif g < 5:
                        issue_w2(g + 1, dcg + 2 - 8, w2_pe_done[(g, dcg)])
                    nc.vector.wait_ge(s_pe.h, w2_pe_done[(g, dcg)])
                    for dc2 in range(2):
                        for half in range(2):
                            dc = dcg * 2 + dc2
                            ev = nc.vector.tensor_tensor(
                                xr[:, dc * T + half * HB:
                                   dc * T + (half + 1) * HB],
                                xr[:, dc * T + half * HB:
                                   dc * T + (half + 1) * HB],
                                pb[setb + dc2 * 2 + half][:, :], ALU.add)
                    w2_ev_done[(g, dcg)] = s_dve.inc(ev)
                if g < 5:
                    gate = w2_ev_done[(g, 7)]
                    build_cB(2 * (g + 1), 0, gate)
                    build_cB(2 * (g + 1) + 1, 1, gate)
            nc.vector.wait_ge(s_dve.h, s_dve.n)
        barrier()

    def attn():
        with contextlib.ExitStack() as ph:
            _u = uid()
            psb = lambda n_, s_, d_: ph.enter_context(nc.psum_tensor(n_ + _u, s_, d_))
            lsb = lambda n_, s_, d_: ph.enter_context(nc.sbuf_tensor(n_ + _u, s_, d_))
            pA = psb("paA", [P, HB], F32)
            pB = psb("paB", [P, HB], F32)
            pZ = [psb("paZ%d" % i2, [P, HB], F32) for i2 in range(2)]
            pD = [psb("paD%d" % i2, [P, HB], F32) for i2 in range(2)]
            pT = [psb("paT%d" % i2, [P, HB], BF16) for i2 in range(2)]
            wkv = [lsb("wkv%d" % i2, [P, NDC * EH], BF16) for i2 in range(2)]
            wst = [lsb("wst%d" % i2, [P, NDC * EH], F32) for i2 in range(2)]
            wob = [lsb("wob%d" % i2, [P, D], BF16) for i2 in range(2)]
            kst = [lsb("kst%d" % i2, [P, T], BF16) for i2 in range(2)]
            kfull = [lsb("kfull%d" % i2, [P, 2 * T], BF16) for i2 in range(2)]
            vE = [lsb("vE%d" % i2, [P, 2 * T], BF16) for i2 in range(2)]
            vt = [lsb("vt%d" % i2, [P, 2 * T], BF16) for i2 in range(2)]
            qsb = [lsb("qsb%d" % i2, [P, T], BF16) for i2 in range(2)]
            expb = [lsb("expb%d" % i2, [P, HB], BF16) for i2 in range(3)]
            ztb = [lsb("ztb%d" % i2, [P, T], BF16) for i2 in range(2)]
            sqs = [lsb("sqs%d" % i2, [P, T], BF16) for i2 in range(2)]
            recb = [lsb("recb%d" % i2, [P, HB], F32) for i2 in range(2)]
            stat = lsb("stat", [P, 16], F32)
            qkmax = lsb("qkmax", [P, 2 * NH], F32)
            negB = lsb("negB", [P, NH], F32)
            def kv_stage(w_dram, bias_s, kv_dram, stat_col0):
                hd0 = s_hd.n
                evac_ms = {}
                square_ms = {}
                statmm_ms = {}

                def emit_square(h2):
                    if h2 >= 2:
                        nc.scalar.wait_ge(s_pe.h, statmm_ms[h2 - 2])
                    aa = nc.scalar.activation(sqs[h2 % 2][:, :],
                                              kst[h2 % 2][:, :], AF.Square)
                    square_ms[h2] = s_act.inc(aa)

                def emit_statmm(h2):
                    nc.tensor.wait_ge(s_act.h, square_ms[h2])
                    if h2 >= 1:
                        nc.tensor.wait_ge(s_dve.h, statred_ms[h2 - 1])
                    for half in range(2):
                        mm2 = nc.tensor.matmul(
                            pZ[half][:, :], ones_b[:, :],
                            sqs[h2 % 2][:, half * HB:(half + 1) * HB],
                            start=True, stop=True)
                    statmm_ms[h2] = s_pe.inc(mm2)
                    nc.vector.wait_ge(s_pe.h, statmm_ms[h2])
                    chain(nc.vector, s_dve,
                          nc.vector.reduce_max(stat[:, 1:2], pZ[0][:, :],
                                               axis=AX))
                    chain(nc.vector, s_dve,
                          nc.vector.reduce_max(stat[:, 2:3], pZ[1][:, :],
                                               axis=AX))
                    chain(nc.vector, s_dve,
                          nc.vector.tensor_tensor(
                              qkmax[:, stat_col0 + h2:stat_col0 + h2 + 1],
                              stat[:, 1:2], stat[:, 2:3], ALU.max))
                    statred_ms[h2] = s_dve.n

                statred_ms = {}
                cast_ms = {}

                def issue_w(h2):
                    wbuf2 = h2 % 2
                    if h2 >= 2:
                        nc.sync.wait_ge(s_pe.h, s_pe.n)
                    i3 = nc.sync.dma_start(wst[wbuf2][:, :],
                                           w_dram[h2 * P:(h2 + 1) * P, :])
                    sw[wbuf2].inc(i3, 16)

                def cast_w(h2):
                    wbuf2 = h2 % 2
                    nc.vector.wait_ge(sw[wbuf2].h, sw[wbuf2].n)
                    if h2 >= 2:
                        nc.vector.wait_ge(s_pe.h, s_pe.n)
                    cc2 = nc.vector.tensor_copy(wkv[wbuf2][:, :],
                                                wst[wbuf2][:, :])
                    cast_ms[h2] = s_dve.inc(cc2)

                issue_w(0)
                issue_w(1)
                cast_w(0)
                for h in range(NH):
                    wbuf = h % 2
                    kb = kst[h % 2]
                    if h + 1 < NH:
                        cast_w(h + 1)
                    nc.tensor.wait_ge(s_dve.h, cast_ms[h])
                    if h >= 1:
                        nc.tensor.wait_ge(s_act.h, evac_ms[h - 1])
                    else:
                        nc.tensor.wait_ge(s_act.h, s_act.n)
                    for half in range(2):
                        pp = pA if half == 0 else pB
                        for dc in range(NDC):
                            mm = nc.tensor.matmul(
                                pp[:, :], wkv[wbuf][:, dc * EH:(dc + 1) * EH],
                                xl[:, dc * T + half * HB:
                                   dc * T + (half + 1) * HB],
                                start=(dc == 0), stop=(dc == NDC - 1))
                        s_pe.inc(mm)
                        nc.scalar.wait_ge(s_pe.h, s_pe.n)
                        if h >= 2 and half == 0:
                            nc.scalar.wait_ge(s_hd.h, hd0 + 16 * (h - 1))
                        aa = nc.scalar.activation(
                            kb[:, half * HB:(half + 1) * HB], pp[:, :],
                            AF.Identity, bias=bias_s[:, h:h + 1])
                        s_act.inc(aa)
                    evac_ms[h] = s_act.n
                    if stat_col0 is not None:
                        emit_square(h)
                    if h + 2 < NH:
                        issue_w(h + 2)
                    nc.sync.wait_ge(s_act.h, evac_ms[h])
                    s_hd.inc(nc.sync.dma_start(
                        kv_dram[h * P:(h + 1) * P, :], kb[:, :]), 16)
                    if stat_col0 is not None and h >= 1:
                        emit_statmm(h - 1)
                if stat_col0 is not None:
                    emit_statmm(NH - 1)

            kv_stage(wk_d, bk_s, kvK_in, NH)
            nc.gpsimd.wait_ge(s_hd.h, s_hd.n)
            s_cc.inc(nc.gpsimd.collective_compute(
                "AllGather", ALU.bypass,
                replica_groups=[[0, 1], [2, 3], [4, 5], [6, 7]],
                ins=[kvK_in.opt()], outs=[kvK_out.opt()]))
            nc.sync.wait_ge(s_dve.h, s_dve.n)
            s_hd.inc(nc.sync.dma_start(kstat_in, qkmax[0:1, NH:2 * NH]), 16)
            nc.gpsimd.wait_ge(s_hd.h, s_hd.n)
            s_cc.inc(nc.gpsimd.collective_compute(
                "AllReduce", ALU.max, replica_groups=[list(range(8))],
                ins=[kstat_in.opt()], outs=[kstat_out.opt()]))
            nc.scalar.wait_ge(s_hd.h, s_hd.n)
            kv_stage(wv_d, bv_s, kvV_in, None)
            nc.gpsimd.wait_ge(s_hd.h, s_hd.n)
            s_cc.inc(nc.gpsimd.collective_compute(
                "AllGather", ALU.bypass,
                replica_groups=[[0, 1], [2, 3], [4, 5], [6, 7]],
                ins=[kvV_in.opt()], outs=[kvV_out.opt()]))
            nc.scalar.wait_ge(s_hd.h, s_hd.n)
            kv_stage(wq_d, bq_s, qstage_d, 0)
            nc.sync.wait_ge(s_cc.h, 2)
            s_hd.inc(nc.sync.dma_start(qkmax[0:1, NH:2 * NH], kstat_out), 16)
            nc.vector.wait_ge(s_hd.h, s_hd.n)
            nc.vector.wait_ge(s_dve.h, s_dve.n)
            chain(nc.vector, s_dve,
                  nc.vector.tensor_tensor(stat[0:1, 0:NH], qkmax[0:1, 0:NH],
                                          qkmax[0:1, NH:2 * NH], ALU.mult))
            chain(nc.vector, s_dve,
                  nc.vector.tensor_scalar_mul(stat[0:1, 0:NH], stat[0:1, 0:NH],
                                              1.0 / 128.0))
            nc.scalar.wait_ge(s_dve.h, s_dve.n)
            chain(nc.scalar, s_act,
                  nc.scalar.activation(recb[0][0:1, 0:NH], stat[0:1, 0:NH],
                                       AF.Ln))
            chain(nc.scalar, s_act,
                  nc.scalar.activation(stat[0:1, 0:NH], recb[0][0:1, 0:NH],
                                       AF.Exp, scale=0.5))
            nc.vector.wait_ge(s_act.h, s_act.n)
            chain(nc.vector, s_dve,
                  nc.vector.tensor_scalar_mul(stat[0:1, 0:NH], stat[0:1, 0:NH],
                                              -1.0))
            nc.tensor.wait_ge(s_dve.h, s_dve.n)
            mm = nc.tensor.matmul(pA[:, 0:NH], ones_f[0:1, :],
                                  stat[0:1, 0:NH], start=True, stop=True)
            s_pe.inc(mm)
            nc.vector.wait_ge(s_pe.h, s_pe.n)
            chain(nc.vector, s_dve,
                  nc.vector.tensor_copy(negB[:, :], pA[:, 0:NH]))

            kv_done = {}
            wo_gate = {}
            pend_wo = []
            zt_done = {}
            exp_done = {}
            msk_done = {}
            avdn_done = {}
            head_pe_done = {}
            q_loaded = {}

            def issue_gathers(h):
                buf = h % 2
                nc.sync.wait_ge(s_cc.h, 1)
                if h >= 2:
                    nc.sync.wait_ge(s_pe.h, head_pe_done[h - 2])
                    nc.gpsimd.wait_ge(s_pe.h, head_pe_done[h - 2])
                for s2 in range(2):
                    i2 = nc.sync.dma_start(
                        kfull[buf][:, s2 * T:(s2 + 1) * T],
                        kvK_out[s2 * D + h * P:s2 * D + (h + 1) * P, :])
                    s_kv.inc(i2, 16)
                nc.gpsimd.wait_ge(s_cc.h, 3)
                for s2 in range(2):
                    i2 = nc.gpsimd.dma_start(
                        vE[buf][:, s2 * T:(s2 + 1) * T],
                        kvV_out[s2 * D + h * P:s2 * D + (h + 1) * P, :])
                    s_kv.inc(i2, 16)
                kv_done[h] = s_kv.n

            def issue_qload(h):
                buf = h % 2
                if h >= 2:
                    nc.sync.wait_ge(s_pe.h, head_pe_done[h - 2])
                i2 = nc.sync.dma_start(qsb[buf][:, :],
                                       qstage_d[h * P:(h + 1) * P, :])
                q_loaded[h] = s_hd.inc(i2, 16)

            issue_gathers(0)
            issue_gathers(1)
            issue_qload(0)
            issue_qload(1)

            for h in range(NH):
                buf = h % 2
                nc.tensor.wait_ge(s_kv.h, kv_done[h])
                if h >= 2:
                    nc.tensor.wait_ge(s_pe.h, head_pe_done[h - 2])
                act_vt0 = s_act.n
                for kt in range(16):
                    pt = pT[kt % 2]
                    if kt >= 2:
                        nc.tensor.wait_ge(s_act.h, act_vt0 + kt - 1)
                    s_pe.inc(nc.tensor.transpose(
                        pt[:, 0:P], vE[buf][:, kt * P:(kt + 1) * P],
                        ident_b[:, :]))
                    nc.scalar.wait_ge(s_pe.h, s_pe.n)
                    s_act.inc(nc.scalar.activation(
                        vt[buf][:, kt * P:(kt + 1) * P], pt[:, 0:P], AF.Copy))

                if pend_wo:
                    g2p, ztg, fn = pend_wo.pop(0)
                    fn(g2p, ztg)

                nc.tensor.wait_ge(s_hd.h, q_loaded[h])
                def emit_sc(it):
                    kt, half = divmod(it, 2)
                    psc = pA if it % 2 == 0 else pB
                    if it >= 2:
                        nc.tensor.wait_ge(s_act.h, exp_done[(h, it - 2)])
                    else:
                        nc.tensor.wait_ge(s_dve.h, s_dve.n)
                    mm = nc.tensor.matmul(
                        psc[:, :], kfull[buf][:, kt * P:(kt + 1) * P],
                        qsb[buf][:, half * HB:(half + 1) * HB],
                        start=True, stop=False)
                    g0 = 2048 - kt * P + half * HB
                    mm = nc.tensor.matmul(psc[:, :], ident_b[:, :],
                                          bigm[:, g0:g0 + HB],
                                          start=False, stop=True)
                    pe_n = s_pe.inc(mm)
                    nc.scalar.wait_ge(s_pe.h, pe_n)
                    if it >= 3:
                        nc.scalar.wait_ge(s_pe.h, avdn_done[(h, it - 3)])
                    a2 = nc.scalar.activation(expb[it % 3][:, :], psc[:, :],
                                              AF.Exp, scale=SCALE_QK,
                                              bias=negB[:, h:h + 1])
                    exp_done[(h, it)] = s_act.inc(a2)

                def emit_av(it):
                    kt, half = divmod(it, 2)
                    nc.tensor.wait_ge(s_act.h, exp_done[(h, it)])
                    mm = nc.tensor.matmul(
                        pZ[half][:, :], vt[buf][:, kt * P:(kt + 1) * P],
                        expb[it % 3][:, :],
                        start=(kt == 0), stop=(kt == 15))
                    mm2 = nc.tensor.matmul(
                        pD[half][:, :], ones_b[:, :], expb[it % 3][:, :],
                        start=(kt == 0), stop=(kt == 15))
                    avdn_done[(h, it)] = s_pe.inc(mm2)

                for it in range(32):
                    emit_sc(it)
                    if it >= 2:
                        emit_av(it - 2)
                emit_av(30)
                emit_av(31)
                head_pe_done[h] = s_pe.n
                if h + 2 < NH:
                    issue_gathers(h + 2)
                    issue_qload(h + 2)
                nc.vector.wait_ge(s_pe.h, head_pe_done[h])
                for half in range(2):
                    chain(nc.vector, s_dve,
                          nc.vector.reciprocal(recb[half][:, :],
                                               pD[half][:, :]))
                    s_dve.inc(nc.vector.tensor_tensor(
                        ztb[h % 2][:, half * HB:(half + 1) * HB],
                        pZ[half][:, :], recb[half][:, :], ALU.mult))
                zt_done[h] = s_dve.n

                if h % 2 == 0:
                    g2n = h // 2
                    for hh in range(2):
                        if g2n >= 1:
                            nc.gpsimd.wait_ge(s_pe.h, wo_gate[g2n - 1])
                        i2 = nc.gpsimd.dma_start(wob[hh][:, :],
                                                 wo_d[g2n * 2 + hh])
                        sw[4 + hh].inc(i2, 16)
                def emit_wo(g2, zt_gate):
                    nc.tensor.wait_ge(s_dve.h, zt_gate)
                    nc.tensor.wait_ge(sw[4].h, sw[4].n)
                    nc.tensor.wait_ge(sw[5].h, sw[5].n)
                    rot = [pA, pB, pZ[0], pZ[1]]
                    ev_ms = {}
                    for dc in range(NDC):
                        for half in range(2):
                            pr2 = rot[(dc * 2 + half) % 4]
                            if dc * 2 + half >= 4:
                                nc.tensor.wait_ge(s_dve.h,
                                                  ev_ms[dc * 2 + half - 4])
                            for hh in range(2):
                                mm = nc.tensor.matmul(
                                    pr2[:, :],
                                    wob[hh][:, dc * P:(dc + 1) * P],
                                    ztb[hh][:, half * HB:(half + 1) * HB],
                                    start=(hh == 0), stop=(hh == 1))
                            s_pe.inc(mm)
                            nc.vector.wait_ge(s_pe.h, s_pe.n)
                            if g2 == 0:
                                ev = nc.vector.scalar_tensor_tensor(
                                    xr[:, dc * T + half * HB:
                                       dc * T + (half + 1) * HB],
                                    pr2[:, :], bo_s[:, dc:dc + 1],
                                    xr[:, dc * T + half * HB:
                                       dc * T + (half + 1) * HB],
                                    ALU.add, ALU.add)
                            else:
                                ev = nc.vector.tensor_tensor(
                                    xr[:, dc * T + half * HB:
                                       dc * T + (half + 1) * HB],
                                    xr[:, dc * T + half * HB:
                                       dc * T + (half + 1) * HB],
                                    pr2[:, :], ALU.add)
                            ev_ms[dc * 2 + half] = s_dve.inc(ev)
                    wo_gate[g2] = s_pe.n

                if h % 2 == 1:
                    pend_wo.append((h // 2, zt_done[h], emit_wo))
            g2l, ztg, fn = pend_wo.pop(0)
            fn(g2l, ztg)
            nc.vector.wait_ge(s_dve.h, s_dve.n)
        barrier()

    layer_norm()
    ffn_call()
    if n_sub >= 2:
        layer_norm()
        attn()
    if n_sub >= 3:
        layer_norm()
        ffn_call()

    for s in sems.values():
        if s.n:
            nc.sync.wait_ge(s.h, s.n)
    out = nc.sync.dma_start(
        y_d.rearrange("(dc p) t -> p dc t", p=P),
        xr[:, :].rearrange("p (dc t) -> p dc t", t=T))
    s_out.inc(out, 16)
    nc.sync.wait_ge(s_out.h, s_out.n)
    st.close()
    return nc


def _stage_inputs(inputs):
    f32 = np.float32
    asf = lambda a: np.ascontiguousarray(np.asarray(a, dtype=f32))
    x = asf(inputs["L0_attn_x"])
    w1T = asf(inputs["L0_ffn_w1"]).transpose(0, 2, 1)
    sguT = asf(inputs["L0_ffn_shared_gate_up_w"]).T

    def pack_w1(col0):
        a = w1T[:, :, col0:col0 + IEX].reshape(8, NDC, P, NIC, P)
        a = a.transpose(0, 3, 2, 1, 4).reshape(8 * NIC * P, NDC * P)
        b = sguT[:, col0 // IEX * ISH:col0 // IEX * ISH + ISH]
        b = b.reshape(NDC, P, 4, NIC, P).transpose(2, 3, 1, 0, 4)
        return np.ascontiguousarray(
            np.concatenate([a, b.reshape(4 * NIC * P, NDC * P)], axis=0))

    w2T = asf(inputs["L0_ffn_w2"]).transpose(0, 2, 1)
    a2 = w2T.reshape(8, NIC, P, 8, 256).transpose(0, 3, 2, 1, 4)
    sdT = asf(inputs["L0_ffn_shared_down_w"]).T
    b2 = sdT.reshape(4, NIC, P, 8, 256).transpose(0, 3, 2, 1, 4)
    w2p = np.ascontiguousarray(np.concatenate(
        [a2.reshape(8 * 8 * P, NIC * 256), b2.reshape(4 * 8 * P, NIC * 256)],
        axis=0))

    def pack_qkv(w):
        a = asf(w).reshape(NH, NDC, P, EH).transpose(0, 2, 1, 3)
        return np.ascontiguousarray(a.reshape(NH * P, NDC * EH))

    shared = {
        "w1gp": pack_w1(0),
        "w1up": pack_w1(IEX),
        "w2p": w2p,
        "gw9T": np.ascontiguousarray(
            np.concatenate([asf(inputs["L0_ffn_gate_w"]),
                            asf(inputs["L0_ffn_shared_gate_w"]),
                            np.zeros((7, D), f32)], axis=0).T),
        "wqp": pack_qkv(inputs["L0_attn_W_Q"]),
        "wkp": pack_qkv(inputs["L0_attn_W_K"]),
        "wvp": pack_qkv(inputs["L0_attn_W_V"]),
        "wo": asf(inputs["L0_attn_W_O"]),
        "bqT": np.ascontiguousarray(asf(inputs["L0_attn_b_Q"]).T),
        "bkT": np.ascontiguousarray(asf(inputs["L0_attn_b_K"]).T),
        "bvT": np.ascontiguousarray(asf(inputs["L0_attn_b_V"]).T),
        "bo": np.ascontiguousarray(asf(inputs["L0_attn_b_O"]).reshape(NDC, P).T),
    }
    per_core = []
    for r in range(8):
        b, hh = r // 2, r % 2
        shard = np.ascontiguousarray(x[b, hh * T:(hh + 1) * T, :].T)
        idx = np.empty((P, 2 * NH), np.int32)
        for h in range(NH):
            idx[:, 2 * h + 0] = h * P + np.arange(P)
            idx[:, 2 * h + 1] = D + h * P + np.arange(P)
        m = {"xT": shard, "qbase": np.array([[hh * T]], f32), "kvidx": idx}
        m.update(shared)
        per_core.append(m)
    return per_core


def kernel(**inputs):
    per_core = _stage_inputs(inputs)
    nc = build_program(n_sub=3)
    res = run_bass_kernel_spmd(nc, per_core, core_ids=list(range(8)))
    out = np.empty((4, 2048, D), np.float32)
    for r in range(8):
        b, hh = r // 2, r % 2
        out[b, hh * T:(hh + 1) * T, :] = np.asarray(res.results[r]["y"]).T
    return out
